# version 05 (final)
# baseline (speedup 1.0000x reference)
import contextlib
import numpy as np

import concourse.bass as bass
import concourse.mybir as mybir
from concourse.bass_utils import run_bass_kernel_spmd
from concourse.bass_core_rust import IndirectOffsetOnAxis

F32 = mybir.dt.float32
I32 = mybir.dt.int32
U32 = mybir.dt.uint32
AF = mybir.ActivationFunctionType
ALU = mybir.AluOpType
AX = mybir.AxisListType
ds = bass.ds

B, S, D = 4, 4096, 1024
SH = S // 2
NBLK = 32
NT = 16
H0, HD0 = 16, 64
H1, DK1 = 8, 128
FFN = 4096
NE = 8
CAP = 4096
EPS32 = float(np.finfo(np.float32).eps)

PAIR_GROUPS = [[0, 1], [2, 3], [4, 5], [6, 7]]
ALL_GROUP = [list(range(8))]

L0_LEN = [32 - 2 * i for i in range(16)]
L0_QB = {0: [31 - 2 * i for i in range(16)], 1: [30 - 2 * i for i in range(16)]}
X1_POS = {}
for h_ in (0, 1):
    for i_, j_ in enumerate(L0_QB[h_]):
        X1_POS[j_] = (h_, i_)

NPHASE = 99
DEBUG = []


class Tracker:

    ENGS = ("sync", "tensor", "vector", "scalar", "gpsimd")

    def __init__(self, p):
        self.p = p
        self.ops = {e: [] for e in self.ENGS}
        self.cnt = {"tensor": 0, "vector": 0, "scalar": 0, "gcol": 0}
        self.sems = {e: p.sem("tk_" + e) for e in ("tensor", "vector", "scalar")}
        self.sems["gcol"] = p.sem("tk_gcol")
        self.dsem = {}
        self.last_w = {}
        self.readers = {}
        self.drain_at = {e: 0 for e in ("vector", "scalar")}
        self.waited = {e: {} for e in self.ENGS}
        self.engcnt = {e: 0 for e in self.ENGS}

    def _dsem(self, key):
        if key not in self.dsem:
            self.dsem[key] = [self.p.sem(f"d_{len(self.dsem)}"), 0]
        return self.dsem[key]

    def _need(self, pre, eng, dep):
        dtag, dcnt = dep
        if dtag == eng and eng in ("vector", "scalar"):
            if self.drain_at[eng] < dcnt:
                pre.append(("drain", None, None))
                self.drain_at[eng] = self.cnt[eng]
            return
        if dtag == eng and eng == "tensor":
            return
        if dtag.startswith("eng:"):
            return
        if self.waited[eng].get(dtag, 0) >= dcnt:
            return
        if dtag.startswith("dma:"):
            sem = self._dsem(dtag[4:])[0]
        else:
            sem = self.sems[dtag]
        pre.append(("wait", sem, dcnt))
        self.waited[eng][dtag] = dcnt

    def emit(self, eng, fn, reads=(), writes=(), collective=False, is_dma=False):
        pre = []
        for k in reads:
            if k in self.last_w:
                self._need(pre, eng, self.last_w[k])
        for k in writes:
            if k in self.last_w:
                self._need(pre, eng, self.last_w[k])
            for r in self.readers.get(k, ()):
                self._need(pre, eng, r)
        if is_dma:
            key0 = writes[0] if writes else (list(reads)[0] if reads else "anon")
            ds_ = self._dsem(key0)
            ds_[1] += 16
            sem, step, tag, mycnt = ds_[0], 16, "dma:" + key0, ds_[1]
        elif collective:
            self.cnt["gcol"] += 1
            sem, step, tag, mycnt = self.sems["gcol"], 1, "gcol", self.cnt["gcol"]
        elif eng in ("tensor", "vector", "scalar"):
            self.cnt[eng] += 1
            sem, step, tag, mycnt = self.sems[eng], 1, eng, self.cnt[eng]
        else:
            ds_ = self._dsem("eng_" + eng)
            ds_[1] += 1
            sem, step, tag, mycnt = ds_[0], 1, "dma:eng_" + eng, ds_[1]
        for k in writes:
            self.last_w[k] = (tag, mycnt)
            self.readers[k] = []
        for k in reads:
            self.readers.setdefault(k, []).append((tag, mycnt))
        self.engcnt[eng] += 1

        def body(e, _pre=tuple(pre), _fn=fn, _sem=sem, _step=step):
            for kind, s_, v in _pre:
                if kind == "drain":
                    e.drain()
                else:
                    e.wait_ge(s_, v)
            _fn(e, _sem, _step)
        self.ops[eng].append(body)

    def fence(self):
        finals = [(k, self.sems[k], self.cnt[k]) for k in ("tensor", "vector", "scalar")
                  if self.cnt[k] > 0]
        if self.cnt["gcol"] > 0:
            finals.append(("gcol", self.sems["gcol"], self.cnt["gcol"]))
        for key, (sem, cnt) in self.dsem.items():
            if cnt > 0:
                finals.append(("dma:" + key, sem, cnt))
        for eng in self.ENGS:
            todo = [(t, sem, v) for (t, sem, v) in finals
                    if self.waited[eng].get(t, 0) < v and not (t == eng)]
            for t, sem, v in todo:
                self.waited[eng][t] = v

            def body(e, _f=tuple((sem, v) for (t, sem, v) in todo)):
                for s_, v_ in _f:
                    e.wait_ge(s_, v_)
            self.ops[eng].append(body)

    def finalize(self):
        finals = [(self.sems[e], self.cnt[e]) for e in ("tensor", "vector", "scalar")
                  if self.cnt[e] > 0]
        if self.cnt["gcol"] > 0:
            finals.append((self.sems["gcol"], self.cnt["gcol"]))
        for key, (sem, cnt) in self.dsem.items():
            if cnt > 0:
                finals.append((sem, cnt))

        def body(e, _f=tuple(finals)):
            for s_, v in _f:
                e.wait_ge(s_, v)
        self.ops["gpsimd"].append(body)

    def realize(self, nc):
        ops = self.ops
        with nc.Block() as block:
            @block.sync
            def _(e):
                for f in ops["sync"]:
                    f(e)

            @block.tensor
            def _(e):
                for f in ops["tensor"]:
                    f(e)

            @block.vector
            def _(e):
                for f in ops["vector"]:
                    f(e)

            @block.scalar
            def _(e):
                for f in ops["scalar"]:
                    f(e)

            @block.gpsimd
            def _(e):
                for f in ops["gpsimd"]:
                    f(e)


class Prog:
    def __init__(self):
        self.nc = bass.Bass("TRN2", target_bir_lowering=False)
        self.stack = contextlib.ExitStack()

    def dram(self, name, shape, dtype=F32, kind="Internal"):
        return self.nc.dram_tensor(name, shape, dtype, kind=kind).ap()

    def sbuf(self, name, shape, dtype=F32):
        return self.stack.enter_context(self.nc.sbuf_tensor(name, shape, dtype))

    def psum(self, name, shape, dtype=F32):
        return self.stack.enter_context(self.nc.psum_tensor(name, shape, dtype))

    def sem(self, name):
        return self.stack.enter_context(self.nc.semaphore(name))


def host_constants():
    c = {}
    c["ident"] = np.eye(128, dtype=np.float32)
    inv = 1.0 / (10000.0 ** (np.arange(0, HD0, 2)[: HD0 // 2].astype(np.float32) / HD0))
    ang = np.outer(np.arange(S, dtype=np.float32), inv)
    cosT = np.cos(ang).astype(np.float32)
    sinT = np.sin(ang).astype(np.float32)
    rows = (np.arange(128) % 64) // 2
    c["ropec"] = np.ascontiguousarray(cosT[:, rows].T)
    sgn = np.where(np.arange(128) % 2 == 0, -1.0, 1.0).astype(np.float32)
    c["ropes"] = np.ascontiguousarray(sinT[:, rows].T * sgn[:, None])
    c["t0"] = (np.arange(128)[None, :] - np.arange(128)[:, None]).astype(np.float32)
    for dlt in (-1, 0, 1):
        i = np.arange(128)[:, None]
        k = np.arange(128)[None, :] + 128 * dlt
        c[f"band{dlt+1}"] = np.where(np.abs(i - k) <= 32, 0.0, -2.0e9).astype(np.float32)
    c["iota8"] = np.tile(np.arange(8, dtype=np.float32), (128, 1))
    c["tril"] = (np.arange(128)[:, None] <= np.arange(128)[None, :]).astype(np.float32)
    c["onesc"] = np.ones((128, 1), np.float32)
    c["eps10"] = np.full((128, 1), 1.0e-10, np.float32)
    c["triu1"] = (np.arange(128)[:, None] > np.arange(128)[None, :]).astype(np.float32)
    c["zeros"] = np.zeros((128, 1024), np.float32)
    return c


IX = {}
IX.update({f"x1o{i}": i for i in range(16)})
IX.update({f"slot{g}_{k}": 16 + g * 2 + k for g in range(16) for k in range(2)})
IX["arb"] = 48
IX["arb2"] = 49
IX.update({f"lnq{i}": 50 + i for i in range(16)})
IX.update({f"vh{t}": 66 + t for t in range(18)})
OF = {}
OF.update({f"dlast{i}": i for i in range(16)})
OF.update({f"dprev{i}": 16 + i for i in range(16)})
OF["bndl"] = 32
OF["bndr"] = 33


def percore_tables(c):
    b, half = c // 2, c % 2
    ix = np.zeros((128, 128), np.int32)
    of = np.zeros(128, np.float32)
    for i in range(16):
        of[OF[f"dlast{i}"]] = 128.0 * ((L0_LEN[i] - 1) - L0_QB[half][i])
        of[OF[f"dprev{i}"]] = 128.0 * ((L0_LEN[i] - 2) - L0_QB[half][i])
        j = half * 16 + i
        sh_, sl_ = X1_POS[j]
        ix[:, IX[f"x1o{i}"]] = sh_ * SH + sl_ * 128 + np.arange(128)
    for g in range(16):
        for k in range(2):
            ix[:, IX[f"slot{g}_{k}"]] = c * CAP + g * 256 + k * 128 + np.arange(128)
    ix[0:8, IX["arb"]] = b * 8 + np.arange(8)
    ix[0:2, IX["arb2"]] = [32 + b, 38]
    for i in range(16):
        ix[:, IX[f"lnq{i}"]] = (half * 16 + i) * 128 + np.arange(128)
    for t in range(18):
        r0 = min(max((half * 16 - 1 + t) * 128, 0), S - 128)
        ix[:, IX[f"vh{t}"]] = r0 + np.arange(128)
    of[OF["bndl"]] = -2.0e9 if half == 0 else 0.0
    of[OF["bndr"]] = -2.0e9 if half == 1 else 0.0
    for j in range(NBLK):
        of[34 + j] = 1.0 if (half * 16 <= j < half * 16 + 16) else 0.0
    return ix, np.tile(of, (128, 1))


def build_program():
    p = Prog()
    nc = p.nc
    T = Tracker(p)

    X0 = p.dram("x0", [S, D], kind="ExternalInput")
    W = {}
    for nm, shp in [
        ("l0a_wq", [D, D]), ("l0a_wk", [D, D]), ("l0a_wv", [D, D]), ("l0a_wo", [D, D]),
        ("l0f_wq", [D, D]), ("l0f_wk", [D, D]), ("l0f_wv", [D, D]), ("l0f_wo", [D, D]),
        ("l0f_wr", [D, NE]),
        ("l1a_wqT", [D, D]), ("l1a_wkT", [D, D]), ("l1a_wvT", [D, D]), ("l1a_woT", [D, D]),
        ("l1a_wprojT", [D, D]), ("l1a_w1T", [D, 512]), ("l1a_w2T", [128, 4]),
        ("l1f_wg", [D, D]),
        ("moe_w1", [D, FFN]), ("moe_w2", [FFN, D]),
        ("l1f_we", [NE * D, D]),
    ]:
        W[nm] = p.dram(nm, shp, kind="ExternalInput")
    for nm, ln in [
        ("l0a_wob", D),
        ("l0f_bq", D), ("l0f_bk", D), ("l0f_bv", D), ("l0f_bo", D),
        ("l0f_g1", D), ("l0f_b1", D), ("l0f_g2", D), ("l0f_b2", D),
        ("l1a_bq", D), ("l1a_bk", D), ("l1a_bv", D), ("l1a_bo", D),
        ("l1a_bproj", D), ("l1a_b1", 512), ("l1a_b2", 1),
        ("l1f_bg", D),
        ("moe_b1", FFN), ("moe_b2", D),
        ("l1f_be", NE * D),
    ]:
        W[nm] = p.dram(nm + "_rep", [128, ln], kind="ExternalInput")
    C = {}
    for nm, shp in [
        ("ident", [128, 128]), ("ropec", [128, S]), ("ropes", [128, S]),
        ("t0", [128, 128]), ("band0", [128, 128]), ("band1", [128, 128]),
        ("band2", [128, 128]), ("iota8", [128, 8]), ("zeros", [128, 1024]),
        ("tril", [128, 128]), ("triu1", [128, 128]), ("onesc", [128, 1]),
        ("eps10", [128, 1]),
    ]:
        C[nm] = p.dram("c_" + nm, shp, kind="ExternalInput")
    OFFF = p.dram("offf", [128, 128], kind="ExternalInput")
    X0Q = p.dram("x0q", [SH, D], kind="ExternalInput")
    RCQ = p.dram("ropecq", [128, SH], kind="ExternalInput")
    RSQ = p.dram("ropesq", [128, SH], kind="ExternalInput")
    IDXT = p.dram("idxt", [128, 128], I32, kind="ExternalInput")
    BQC = p.dram("l0f_bq_col", [128, 8], kind="ExternalInput")
    BKC = p.dram("l0f_bk_col", [128, 8], kind="ExternalInput")
    TOKC = p.dram("tokcol", [128, 128], I32, kind="ExternalInput")
    TOKCF = p.dram("tokcolf", [128, 128], kind="ExternalInput")
    B1CL = p.dram("l1a_b1_col", [128, 4], kind="ExternalInput")

    KF0 = p.dram("KF0", [D, S])
    VT0 = p.dram("VT0", [S, D])
    QF0 = p.dram("QF0", [D, SH])
    OTM = p.dram("OTM", [SH, D])
    C1IN = p.dram("c1in", [SH, D])
    X1F = p.dram("X1F", [2 * SH, D])
    LN1X = p.dram("LN1X", [S, D])
    KF1 = p.dram("KF1", [D, S])
    VT1 = p.dram("VT1", [S, D])
    QF1 = p.dram("QF1", [D, SH])
    XMID = p.dram("XMID", [SH, D])
    C2IN = p.dram("c2in", [SH, D])
    HTAB = p.dram("HTAB", [B * S + 128, D])
    LGIN = p.dram("lgin", [NE, SH])
    LGOUT = p.dram("lgout", [8 * NE, SH])
    SLOT = p.dram("SLOT", [NE * CAP + 256, 2])
    ZTAB = p.dram("ZTAB", [B * S + 128, D])
    MOE = p.dram("MOE", [SH, D])
    C4IN = p.dram("c4in", [SH, D])
    X2F = p.dram("X2F", [2 * SH, D])
    INV = p.dram("INVT", [S, D])
    XQK = p.dram("XQK", [S, D])
    KNT = p.dram("KNT", [S, D])
    KSF = p.dram("KSF", [D, S])
    QNT = p.dram("QNT", [SH, D])
    QSF = p.dram("QSF", [D, SH])
    QT1 = p.dram("QT1", [SH, D])
    KT1 = p.dram("KT1", [S, D])
    VT2 = p.dram("VT2", [18 * 128, D])
    OPRE = p.dram("OPRE", [D, SH])
    X3 = p.dram("X3", [SH, D])
    X3F = p.dram("X3F", [D, SH])
    QSB = p.dram("qsb", [1024]); KSB = p.dram("ksb", [1024]); HJB = p.dram("hjb", [8])
    AR1I = p.dram("ar1i", [16, 128]); AR1O = p.dram("ar1o", [16, 128])
    AR2I = p.dram("ar2i", [64, 128]); AR2O = p.dram("ar2o", [64, 128])
    AR3I = p.dram("ar3i", [16, 128]); AR3O = p.dram("ar3o", [16, 128])
    OUT = p.dram("out", [SH, D], kind="ExternalOutput")
    DBG = {nm: p.dram("dbg_" + nm, [SH, D], kind="ExternalOutput") for nm in DEBUG}

    ident = p.sbuf("identS", [128, 128])
    t0c = p.sbuf("t0cS", [128, 128])
    idxt = p.sbuf("idxtS", [128, 128], I32)
    offf = p.sbuf("offfS", [128, 128])
    ropec = p.sbuf("ropecS", [128, 512])
    ropes = p.sbuf("ropesS", [128, 512])
    xt = [p.sbuf(f"xt{i}", [128, D]) for i in range(4)]
    xf = p.sbuf("xfS", [128, 8 * 512])
    wS = [p.sbuf(f"w{i}S", [128, 8 * D]) for i in range(2)]
    evac = [p.sbuf(f"evac{i}", [128, 512]) for i in range(4)]
    kf = p.sbuf("kfS", [128, S])
    vh = p.sbuf("vhS", [128, 32 * 64])
    qf = p.sbuf("qfS", [128, 128])
    pbuf = p.sbuf("pbufS", [128, S])
    ptr = p.sbuf("ptrS", [128, 256])
    oev = p.sbuf("oevS", [128, 128])
    otm = p.sbuf("otmS", [128, 1024])
    stat = p.sbuf("statS", [128, 32])
    lstat = p.sbuf("lstatS", [128, 16])
    brep = [p.sbuf(f"brep{i}", [128, D]) for i in range(2)]
    scr = p.sbuf("scrS", [128, 1024])
    mtile = p.sbuf("mtileS", [128, 128])
    attn = p.sbuf("attnS", [128, 1024])
    x1buf = p.sbuf("x1bufS", [128, 1024])
    rsw = p.sbuf("rswS", [128, 512])
    brep2 = p.sbuf("brep2S", [128, D])
    iotat = p.sbuf("iotatS", [128, 8])
    tokc2_i32 = p.sbuf("tokc2S", [128, 64], I32)
    bqc = p.sbuf("bqcS", [128, 8])
    bkc = p.sbuf("bkcS", [128, 8])
    tokc = p.sbuf("tokcS", [128, 128], I32)
    tokcf = p.sbuf("tokcfS", [128, 128])
    sidx = p.sbuf("sidxS", [128, 4], I32)
    sgg = p.sbuf("sggS", [128, 4])
    carry = p.sbuf("carryS", [128, 4])
    rstat = p.sbuf("rstatS", [128, 16])
    onescol = p.sbuf("onescolS", [128, 1])
    wrepb = p.sbuf("wrepbS", [128, 4])
    b1cl = p.sbuf("b1clS", [128, 4])
    w2tc = p.sbuf("w2tcS", [128, 4])
    epcol = p.sbuf("epcolS", [128, 32])
    hjrow = p.sbuf("hjrowS", [128, 16])
    bandt = p.sbuf("bandtS", [128, 384])
    eps10 = p.sbuf("eps10S", [128, 1])
    b2s = p.sbuf("b2sS", [128, 1])
    ramu = p.sbuf("ramuS", [128, 8], U32)

    PS = [p.psum(f"psum{i}", [128, 512]) for i in range(8)]
    PK = [f"ps{i}" for i in range(8)]

    def dma(eng, dst, src, rd, wr):
        def f(e, sem, st, dst=dst, src=src):
            e.dma_start(out=dst, in_=src).then_inc(sem, st)
        T.emit(eng, f, reads=rd, writes=wr, is_dma=True)

    def ig(dst, table, idxcol, rd, wr):
        def f(e, sem, st, dst=dst, table=table, idxcol=idxcol):
            e.indirect_dma_start(
                out=dst, out_offset=None, in_=table,
                in_offset=IndirectOffsetOnAxis(ap=idxcol, axis=0)).then_inc(sem, st)
        T.emit("gpsimd", f, reads=list(rd) + ["idxt"], writes=wr, is_dma=True)

    def isc(table, idxcol, src, rd, wr, bounds=None):
        def f(e, sem, st, table=table, idxcol=idxcol, src=src, bounds=bounds):
            kw = {}
            if bounds is not None:
                kw = dict(bounds_check=bounds, oob_is_err=False)
            e.indirect_dma_start(
                out=table, out_offset=IndirectOffsetOnAxis(ap=idxcol, axis=0),
                in_=src, in_offset=None, **kw).then_inc(sem, st)
        T.emit("gpsimd", f, reads=list(rd) + ["idxt"], writes=wr, is_dma=True)

    def mm(bk, out_ap, lhsT_ap, rhs_ap, rd, start, stop, outkey=None):
        def f(e, sem, st, out_ap=out_ap, lhsT_ap=lhsT_ap, rhs_ap=rhs_ap,
              start=start, stop=stop):
            e.matmul(out_ap, lhsT_ap, rhs_ap, start=start, stop=stop).then_inc(sem, st)
        T.emit("tensor", f, reads=list(rd) + ([] if start else [outkey or PK[bk]]),
               writes=[outkey or PK[bk]])

    def tpose(key, out_ap, in_ap, rd, np_=128):
        def f(e, sem, st, out_ap=out_ap, in_ap=in_ap, np_=np_):
            e.transpose(out_ap, in_ap, ident[0:np_, 0:np_]).then_inc(sem, st)
        T.emit("tensor", f, reads=list(rd) + ["ident"], writes=[key])

    def vx(fn, rd, wr, eng="vector"):
        def f(e, sem, st, fn=fn):
            fn(e).then_inc(sem, st)
        T.emit(eng, f, reads=rd, writes=wr)

    def collective(kind, op, groups, in_ap, out_ap, rd, wr):
        def f(e, sem, st, kind=kind, op=op, groups=groups, in_ap=in_ap, out_ap=out_ap):
            e.collective_compute(kind, op, replica_groups=groups,
                                 ins=[in_ap.opt()], outs=[out_ap.opt()]).then_inc(sem, st)
        T.emit("gpsimd", f, reads=rd, writes=wr, collective=True)

    dma("sync", ident[:], C["ident"][:], [], ["ident"])
    dma("sync", t0c[:], C["t0"][:], [], ["t0c"])
    dma("sync", idxt[:], IDXT[:], [], ["idxt"])
    dma("sync", offf[:], OFFF[:], [], ["offf"])

    def fm_group(src_table, bases, srckey, ntile=4):
        for t in range(ntile):
            dma("sync", xt[t][:], src_table[bases[t]:bases[t] + 128, :],
                [srckey], [f"xt{t}"])
        for c8 in range(8):
            bk = 6 + (c8 % 2)
            for t in range(ntile):
                tpose(PK[bk], PS[bk][:, t * 128:(t + 1) * 128],
                      xt[t][:, c8 * 128:(c8 + 1) * 128], [f"xt{t}"])
            vx(lambda e, c8=c8, bk=bk, ntile=ntile: e.tensor_copy(
                xf[:, c8 * 512:c8 * 512 + ntile * 128], PS[bk][:, 0:ntile * 128]),
                [PK[bk]], [f"xf{c8}"])

    def load_w_resident(slot, src, key, width=D):
        dma("sync", wS[slot][:, 0:8 * width].rearrange("p (dc n) -> p dc n", dc=8),
            src.rearrange("(dc p) n -> p dc n", p=128), [key], [f"wS{slot}"])

    def wchunk(slot, dch, c0, c1):
        return wS[slot][:, dch * D + c0: dch * D + c1]

    def layernorm_tile(src_ap, dst_ap, rk, wk):
        vx(lambda e: e.reduce_sum(lstat[:, 0:1], src_ap, axis=AX.X), rk, ["ln_s"])
        vx(lambda e: e.tensor_scalar_mul(lstat[:, 1:2], lstat[:, 0:1], 1.0 / D),
           ["ln_s"], ["ln_m"])
        vx(lambda e: e.activation(scr[:, 0:D], src_ap, AF.Square,
                                  accum_out=lstat[:, 2:3]),
           rk, ["scr", "ln_q"], eng="scalar")
        vx(lambda e: e.tensor_scalar_mul(lstat[:, 3:4], lstat[:, 2:3], 1.0 / D),
           ["ln_q"], ["ln_v1"])
        vx(lambda e: e.tensor_mul(lstat[:, 4:5], lstat[:, 1:2], lstat[:, 1:2]),
           ["ln_m"], ["ln_v2"])
        vx(lambda e: e.tensor_sub(lstat[:, 5:6], lstat[:, 3:4], lstat[:, 4:5]),
           ["ln_v1", "ln_v2"], ["ln_v"])
        vx(lambda e: e.tensor_scalar_add(lstat[:, 6:7], lstat[:, 5:6], 1.0e-5),
           ["ln_v"], ["ln_ve"])
        vx(lambda e: e.activation(lstat[:, 7:8], lstat[:, 6:7], AF.Sqrt),
           ["ln_ve"], ["ln_sd"], eng="scalar")
        vx(lambda e: e.reciprocal(lstat[:, 8:9], lstat[:, 7:8]),
           ["ln_sd"], ["ln_r"])
        vx(lambda e: e.tensor_scalar(dst_ap, src_ap, lstat[:, 1:2], lstat[:, 8:9],
                                     ALU.subtract, ALU.mult),
           rk + ["ln_m", "ln_r"], [wk])

    def phase0():
        load_w_resident(0, W["l0a_wk"], "l0a_wk")
        load_w_resident(1, W["l0a_wv"], "l0a_wv")

        def rope_store(bk, kc, dst_dram, dkey, colbase, width=512):
            ev = evac[kc % 4]
            ek = f"evac{kc % 4}"
            swapm = [(i ^ 1) for i in range(32)]
            vx(lambda e, bk=bk, width=width, swapm=swapm: e.stream_shuffle(
                rsw[:, 0:width], PS[bk][:, 0:width], swapm),
               [PK[bk]], ["rsw"])
            vx(lambda e, ev=ev, bk=bk, width=width: e.tensor_tensor(
                ev[:, 0:width], PS[bk][:, 0:width], ropec[:, 0:width], ALU.mult),
               [PK[bk], "ropec"], [ek])
            vx(lambda e, width=width: e.tensor_tensor(
                rsw[:, 0:width], rsw[:, 0:width], ropes[:, 0:width], ALU.mult),
               ["rsw", "ropes"], ["rsw"])
            vx(lambda e, ev=ev, width=width: e.tensor_add(
                ev[:, 0:width], ev[:, 0:width], rsw[:, 0:width]),
               [ek, "rsw"], [ek])
            dma("gpsimd", dst_dram[kc * 128:(kc + 1) * 128, colbase:colbase + width],
                ev[:, 0:width], [ek], [dkey])

        for g in range(8):
            cb = g * 512
            fm_group(X0, [cb + t * 128 for t in range(4)], "X0")
            dma("sync", ropec[:], C["ropec"][:, cb:cb + 512], [], ["ropec"])
            dma("sync", ropes[:], C["ropes"][:, cb:cb + 512], [], ["ropes"])
            for kc in range(8):
                bk = kc % 2
                for dch in range(8):
                    mm(bk, PS[bk][:], wchunk(0, dch, kc * 128, (kc + 1) * 128),
                       xf[:, dch * 512:(dch + 1) * 512],
                       ["wS0", f"xf{dch}"], start=(dch == 0), stop=(dch == 7))
                rope_store(bk, kc, KF0, "KF0", cb)
            for tsub in range(4):
                for oc in range(2):
                    bk = 2 + (2 * tsub + oc) % 2
                    for dch in range(8):
                        mm(bk, PS[bk][:],
                           xf[:, dch * 512 + tsub * 128: dch * 512 + (tsub + 1) * 128],
                           wchunk(1, dch, oc * 512, (oc + 1) * 512),
                           ["wS1", f"xf{dch}"], start=(dch == 0), stop=(dch == 7))
                    iev = 2 + (2 * tsub + oc) % 2
                    vx(lambda e, iev=iev, bk=bk: e.tensor_copy(evac[iev][:], PS[bk][:]),
                       [PK[bk]], [f"evac{iev}"])
                    dma("gpsimd",
                        VT0[cb + tsub * 128: cb + (tsub + 1) * 128,
                            oc * 512:(oc + 1) * 512],
                        evac[iev][:], [f"evac{iev}"], ["VT0"])

        load_w_resident(0, W["l0a_wq"], "l0a_wq")
        for qg in range(4):
            cb = qg * 512
            fm_group(X0Q, [cb + t * 128 for t in range(4)], "X0Q")
            dma("sync", ropec[:], RCQ[:, cb:cb + 512], [], ["ropec"])
            dma("sync", ropes[:], RSQ[:, cb:cb + 512], [], ["ropes"])
            for kc in range(8):
                bk = kc % 2
                for dch in range(8):
                    mm(bk, PS[bk][:], wchunk(0, dch, kc * 128, (kc + 1) * 128),
                       xf[:, dch * 512:(dch + 1) * 512],
                       ["wS0", f"xf{dch}"], start=(dch == 0), stop=(dch == 7))
                rope_store(bk, kc, QF0, "QF0", cb)

        for h in range(16):
            dma("sync", kf[0:64, :], KF0[h * 64:(h + 1) * 64, :], ["KF0"], ["kf"])
            dma("sync", vh[:].rearrange("p (kb c) -> p kb c", kb=32),
                VT0[:, h * 64:(h + 1) * 64].rearrange("(kb p) c -> p kb c", p=128),
                ["VT0"], ["vh"])
            for i in range(16):
                L = L0_LEN[i]
                ncols = L * 128
                nch = (ncols + 511) // 512
                dma("sync", qf[0:64, :],
                    QF0[h * 64:(h + 1) * 64, i * 128:(i + 1) * 128], ["QF0"], ["qf"])
                for cc in range(nch):
                    w = min(512, ncols - cc * 512)
                    mm(cc, PS[cc][:, 0:w], qf[0:64, :],
                       kf[0:64, cc * 512:cc * 512 + w],
                       ["qf", "kf"], start=True, stop=True)
                for which, kb in (("dprev", L - 2), ("dlast", L - 1)):
                    bkm, col = kb // 4, (kb % 4) * 128
                    oidx = OF[f"{which}{i}"]
                    vx(lambda e, oidx=oidx: e.tensor_scalar(
                        mtile[:], t0c[:], offf[:, oidx:oidx + 1], 0.0,
                        ALU.add, ALU.is_gt), ["t0c", "offf"], ["mtile"])
                    vx(lambda e, bkm=bkm, col=col: e.scalar_tensor_tensor(
                        out=PS[bkm][:, col:col + 128], in0=mtile[:],
                        scalar=-1.0e30, in1=PS[bkm][:, col:col + 128],
                        op0=ALU.mult, op1=ALU.add), ["mtile", PK[bkm]], [PK[bkm]])
                for cc in range(nch):
                    w = min(512, ncols - cc * 512)
                    vx(lambda e, cc=cc, w=w: e.reduce_max(
                        stat[:, cc:cc + 1], PS[cc][:, 0:w], axis=AX.X),
                       [PK[cc]], [f"st{cc}"])
                vx(lambda e, nch=nch: e.reduce_max(
                    stat[:, 8:9], stat[:, 0:nch], axis=AX.X),
                   [f"st{cc}" for cc in range(nch)], ["stM"])
                vx(lambda e: e.tensor_scalar_mul(stat[:, 9:10], stat[:, 8:9], -0.125),
                   ["stM"], ["stB"])
                for cc in range(nch):
                    w = min(512, ncols - cc * 512)
                    vx(lambda e, cc=cc, w=w: e.activation(
                        pbuf[:, cc * 512:cc * 512 + w], PS[cc][:, 0:w], AF.Exp,
                        bias=stat[:, 9:10], scale=0.125,
                        accum_out=stat[:, 16 + cc:17 + cc]),
                       [PK[cc], "stB"], [f"pb{cc}", f"sl{cc}"], eng="scalar")
                vx(lambda e, nch=nch: e.reduce_sum(
                    stat[:, 10:11], stat[:, 16:16 + nch], axis=AX.X),
                   [f"sl{cc}" for cc in range(nch)], ["stL"])
                vx(lambda e: e.reciprocal(stat[:, 11:12], stat[:, 10:11]),
                   ["stL"], ["stR"])
                for g8 in range(0, L, 8):
                    n8 = min(8, L - g8)
                    pbks = sorted(set([f"pb{(g8 + j) // 4}" for j in range(n8)]))
                    for j in range(n8):
                        kb = g8 + j
                        tpose(PK[1 + j // 4], PS[1 + j // 4][:, (j % 4) * 128:(j % 4 + 1) * 128],
                              pbuf[:, kb * 128:(kb + 1) * 128], pbks)
                    for hv in range((n8 + 3) // 4):
                        w8 = min(512, (n8 - hv * 4) * 128)
                        vx(lambda e, hv=hv, w8=w8: e.tensor_copy(
                            scr[:, hv * 512:hv * 512 + w8], PS[1 + hv][:, 0:w8]),
                           [PK[1 + hv]], ["ptg"])
                    for j in range(n8):
                        kb = g8 + j
                        mm(0, PS[0][0:64, 0:128], vh[:, kb * 64:(kb + 1) * 64],
                           scr[:, j * 128:(j + 1) * 128],
                           ["vh", "ptg"], start=(kb == 0), stop=(kb == L - 1))
                vx(lambda e: e.tensor_copy(oev[0:64, :], PS[0][0:64, 0:128]),
                   [PK[0]], ["oev"])
                tpose(PK[3], PS[3][:, 0:64], oev[0:64, :], ["oev"], np_=64)
                vx(lambda e: e.tensor_scalar(
                    otm[:, 0:64], PS[3][:, 0:64], stat[:, 11:12], None,
                    ALU.mult, ALU.bypass), [PK[3], "stR"], ["otm"])
                dma("gpsimd", OTM[i * 128:(i + 1) * 128, h * 64:(h + 1) * 64],
                    otm[:, 0:64], ["otm"], ["OTM"])

        T.fence()
        load_w_resident(0, W["l0a_wo"], "l0a_wo")
        dma("sync", brep[0][:], W["l0a_wob"][:], [], ["brep0"])
        for qg in range(4):
            steps = [qg * 4 + t for t in range(4)]
            fm_group(OTM, [i * 128 for i in steps], "OTM")
            for tsub in range(4):
                i = steps[tsub]
                for oc in range(2):
                    bk = (2 * tsub + oc) % 4
                    for dch in range(8):
                        mm(bk, PS[bk][:],
                           xf[:, dch * 512 + tsub * 128: dch * 512 + (tsub + 1) * 128],
                           wchunk(0, dch, oc * 512, (oc + 1) * 512),
                           ["wS0", f"xf{dch}"], start=(dch == 0), stop=(dch == 7))
                    vx(lambda e, oc=oc, bk=bk: e.tensor_copy(
                        attn[:, oc * 512:(oc + 1) * 512], PS[bk][:]),
                       [PK[bk]], ["attn"])
                dma("sync", xt[0][:], X0Q[i * 128:(i + 1) * 128, :], ["X0Q"], ["xt0"])
                vx(lambda e: e.tensor_add(attn[:], attn[:], brep[0][:]),
                   ["attn", "brep0"], ["attn"])
                vx(lambda e: e.tensor_add(attn[:], attn[:], xt[0][:]),
                   ["attn", "xt0"], ["attn"])
                layernorm_tile(attn[:], x1buf[:], ["attn"], "x1t")
                dma("gpsimd", C1IN[i * 128:(i + 1) * 128, :], x1buf[:],
                    ["x1t"], ["c1in"])
                if "x1own" in DBG:
                    dma("gpsimd", DBG["x1own"][i * 128:(i + 1) * 128, :],
                        x1buf[:], ["x1t"], ["dbgx1"])
        collective("AllGather", ALU.bypass, PAIR_GROUPS, C1IN[:, :], X1F[:, :],
                   ["c1in"], ["X1F"])


    def phase1():
        T.fence()
        dma("sync", bqc[:], BQC[:], [], ["bqc"])
        dma("sync", bkc[:], BKC[:], [], ["bkc"])
        dma("sync", brep[0][:], W["l0f_g1"][:], [], ["brep0"])
        dma("sync", brep[1][:], W["l0f_b1"][:], [], ["brep1"])
        for j in range(NBLK):
            sh_, sl_ = X1_POS[j]
            r0 = sh_ * SH + sl_ * 128
            dma("sync", xt[0][:], X1F[r0:r0 + 128, :], ["X1F"], ["xt0"])
            layernorm_tile(xt[0][:], attn[:], ["xt0"], "lnt")
            vx(lambda e: e.tensor_mul(attn[:], attn[:], brep[0][:]),
               ["lnt", "brep0"], ["lnt"])
            vx(lambda e: e.tensor_add(attn[:], attn[:], brep[1][:]),
               ["lnt", "brep1"], ["lnt"])
            dma("gpsimd", LN1X[j * 128:(j + 1) * 128, :], attn[:], ["lnt"], ["LN1X"])

        load_w_resident(0, W["l0f_wk"], "l0f_wk")
        load_w_resident(1, W["l0f_wv"], "l0f_wv")
        dma("sync", brep[0][:], W["l0f_bv"][:], [], ["brep0"])
        for g in range(8):
            cb = g * 512
            fm_group(LN1X, [cb + t * 128 for t in range(4)], "LN1X")
            for kc in range(8):
                bk = kc % 2
                for dch in range(8):
                    mm(bk, PS[bk][:], wchunk(0, dch, kc * 128, (kc + 1) * 128),
                       xf[:, dch * 512:(dch + 1) * 512],
                       ["wS0", f"xf{dch}"], start=(dch == 0), stop=(dch == 7))
                iev = kc % 4
                vx(lambda e, iev=iev, bk=bk, kc=kc: e.tensor_scalar(
                    evac[iev][:], PS[bk][:], bkc[:, kc:kc + 1], None,
                    ALU.add, ALU.bypass),
                   [PK[bk], "bkc"], [f"evac{iev}"])
                dma("gpsimd", KF1[kc * 128:(kc + 1) * 128, cb:cb + 512],
                    evac[iev][:], [f"evac{iev}"], ["KF1"])
            for tsub in range(4):
                for oc in range(2):
                    bk = 2 + (2 * tsub + oc) % 2
                    for dch in range(8):
                        mm(bk, PS[bk][:],
                           xf[:, dch * 512 + tsub * 128: dch * 512 + (tsub + 1) * 128],
                           wchunk(1, dch, oc * 512, (oc + 1) * 512),
                           ["wS1", f"xf{dch}"], start=(dch == 0), stop=(dch == 7))
                    iev = 2 + (2 * tsub + oc) % 2
                    vx(lambda e, iev=iev, bk=bk, oc=oc: e.tensor_add(
                        evac[iev][:], PS[bk][:], brep[0][:, oc * 512:(oc + 1) * 512]),
                       [PK[bk], "brep0"], [f"evac{iev}"])
                    dma("gpsimd",
                        VT1[cb + tsub * 128: cb + (tsub + 1) * 128,
                            oc * 512:(oc + 1) * 512],
                        evac[iev][:], [f"evac{iev}"], ["VT1"])
        load_w_resident(0, W["l0f_wq"], "l0f_wq")
        for qg in range(4):
            for t in range(4):
                i = qg * 4 + t
                ig(xt[t][:], LN1X, idxt[:, IX[f"lnq{i}"]:IX[f"lnq{i}"] + 1],
                   ["LN1X"], [f"xt{t}"])
            for c8 in range(8):
                bk = 6 + (c8 % 2)
                for t in range(4):
                    tpose(PK[bk], PS[bk][:, t * 128:(t + 1) * 128],
                          xt[t][:, c8 * 128:(c8 + 1) * 128], [f"xt{t}"])
                vx(lambda e, c8=c8, bk=bk: e.tensor_copy(
                    xf[:, c8 * 512:(c8 + 1) * 512], PS[bk][:]), [PK[bk]], [f"xf{c8}"])
            for kc in range(8):
                bk = kc % 2
                for dch in range(8):
                    mm(bk, PS[bk][:], wchunk(0, dch, kc * 128, (kc + 1) * 128),
                       xf[:, dch * 512:(dch + 1) * 512],
                       ["wS0", f"xf{dch}"], start=(dch == 0), stop=(dch == 7))
                iev = kc % 4
                vx(lambda e, iev=iev, bk=bk, kc=kc: e.tensor_scalar(
                    evac[iev][:], PS[bk][:], bqc[:, kc:kc + 1], None,
                    ALU.add, ALU.bypass),
                   [PK[bk], "bqc"], [f"evac{iev}"])
                dma("gpsimd", QF1[kc * 128:(kc + 1) * 128, qg * 512:(qg + 1) * 512],
                    evac[iev][:], [f"evac{iev}"], ["QF1"])

        for h in range(16):
            dma("sync", kf[0:64, :], KF1[h * 64:(h + 1) * 64, :], ["KF1"], ["kf"])
            dma("sync", vh[:].rearrange("p (kb c) -> p kb c", kb=32),
                VT1[:, h * 64:(h + 1) * 64].rearrange("(kb p) c -> p kb c", p=128),
                ["VT1"], ["vh"])
            for i in range(16):
                dma("sync", qf[0:64, :],
                    QF1[h * 64:(h + 1) * 64, i * 128:(i + 1) * 128], ["QF1"], ["qf"])
                for cc in range(8):
                    mm(cc, PS[cc][:], qf[0:64, :], kf[0:64, cc * 512:(cc + 1) * 512],
                       ["qf", "kf"], start=True, stop=True)
                for cc in range(8):
                    vx(lambda e, cc=cc: e.reduce_max(
                        stat[:, cc:cc + 1], PS[cc][:], axis=AX.X),
                       [PK[cc]], [f"st{cc}"])
                vx(lambda e: e.reduce_max(stat[:, 8:9], stat[:, 0:8], axis=AX.X),
                   [f"st{cc}" for cc in range(8)], ["stM"])
                vx(lambda e: e.tensor_scalar_mul(stat[:, 9:10], stat[:, 8:9], -0.125),
                   ["stM"], ["stB"])
                for cc in range(8):
                    vx(lambda e, cc=cc: e.activation(
                        pbuf[:, cc * 512:(cc + 1) * 512], PS[cc][:], AF.Exp,
                        bias=stat[:, 9:10], scale=0.125,
                        accum_out=stat[:, 16 + cc:17 + cc]),
                       [PK[cc], "stB"], [f"pb{cc}", f"sl{cc}"], eng="scalar")
                vx(lambda e: e.reduce_sum(stat[:, 10:11], stat[:, 16:24], axis=AX.X),
                   [f"sl{cc}" for cc in range(8)], ["stL"])
                vx(lambda e: e.reciprocal(stat[:, 11:12], stat[:, 10:11]),
                   ["stL"], ["stR"])
                for g8 in range(0, 32, 8):
                    pbks = sorted(set([f"pb{(g8 + j) // 4}" for j in range(8)]))
                    for j in range(8):
                        kb = g8 + j
                        tpose(PK[1 + j // 4], PS[1 + j // 4][:, (j % 4) * 128:(j % 4 + 1) * 128],
                              pbuf[:, kb * 128:(kb + 1) * 128], pbks)
                    for hv in range(2):
                        vx(lambda e, hv=hv: e.tensor_copy(
                            scr[:, hv * 512:(hv + 1) * 512], PS[1 + hv][:, 0:512]),
                           [PK[1 + hv]], ["ptg"])
                    for j in range(8):
                        kb = g8 + j
                        mm(0, PS[0][0:64, 0:128], vh[:, kb * 64:(kb + 1) * 64],
                           scr[:, j * 128:(j + 1) * 128],
                           ["ptg", "vh"], start=(kb == 0), stop=(kb == 31))
                vx(lambda e: e.tensor_copy(oev[0:64, :], PS[0][0:64, 0:128]),
                   [PK[0]], ["oev"])
                tpose(PK[3], PS[3][:, 0:64], oev[0:64, :], ["oev"], np_=64)
                vx(lambda e: e.tensor_scalar(
                    otm[:, 0:64], PS[3][:, 0:64], stat[:, 11:12], None,
                    ALU.mult, ALU.bypass), [PK[3], "stR"], ["otm"])
                dma("gpsimd", OTM[i * 128:(i + 1) * 128, h * 64:(h + 1) * 64],
                    otm[:, 0:64], ["otm"], ["OTM"])

        T.fence()
        load_w_resident(0, W["l0f_wo"], "l0f_wo")
        dma("sync", brep[0][:], W["l0f_bo"][:], [], ["brep0"])
        dma("sync", brep[1][:], W["l0f_g2"][:], [], ["brep1"])
        dma("sync", brep2[:], W["l0f_b2"][:], [], ["brep2"])
        dma("sync", wS[1][:, 0:64].rearrange("p (dc n) -> p dc n", dc=8),
            W["l0f_wr"].rearrange("(dc p) n -> p dc n", p=128), ["l0f_wr"], ["wS1"])
        for qg in range(4):
            steps = [qg * 4 + t for t in range(4)]
            fm_group(OTM, [i * 128 for i in steps], "OTM")
            for tsub in range(4):
                i = steps[tsub]
                for oc in range(2):
                    bk = (2 * tsub + oc) % 4
                    for dch in range(8):
                        mm(bk, PS[bk][:],
                           xf[:, dch * 512 + tsub * 128: dch * 512 + (tsub + 1) * 128],
                           wchunk(0, dch, oc * 512, (oc + 1) * 512),
                           ["wS0", f"xf{dch}"], start=(dch == 0), stop=(dch == 7))
                    vx(lambda e, oc=oc, bk=bk: e.tensor_copy(
                        attn[:, oc * 512:(oc + 1) * 512], PS[bk][:]),
                       [PK[bk]], ["attn"])
                ig(xt[0][:], X1F, idxt[:, IX[f"x1o{i}"]:IX[f"x1o{i}"] + 1],
                   ["X1F"], ["xt0"])
                vx(lambda e: e.tensor_add(attn[:], attn[:], brep[0][:]),
                   ["attn", "brep0"], ["attn"])
                vx(lambda e: e.tensor_add(attn[:], attn[:], xt[0][:]),
                   ["attn", "xt0"], ["attn"])
                dma("gpsimd", XMID[i * 128:(i + 1) * 128, :], attn[:],
                    ["attn"], ["XMID"])
                if "xmidown" in DBG:
                    dma("gpsimd", DBG["xmidown"][i * 128:(i + 1) * 128, :], attn[:],
                        ["attn"], ["dbgxm"])
                layernorm_tile(attn[:], x1buf[:], ["attn"], "h2t")
                vx(lambda e: e.tensor_mul(x1buf[:], x1buf[:], brep[1][:]),
                   ["h2t", "brep1"], ["h2t"])
                vx(lambda e: e.tensor_add(x1buf[:], x1buf[:], brep2[:]),
                   ["h2t", "brep2"], ["h2t"])
                dma("gpsimd", C2IN[i * 128:(i + 1) * 128, :], x1buf[:],
                    ["h2t"], ["c2in"])
                if "h2own" in DBG:
                    dma("gpsimd", DBG["h2own"][i * 128:(i + 1) * 128, :], x1buf[:],
                        ["h2t"], ["dbgh2"])
                for c8 in range(8):
                    bkx = 6 + (c8 % 2)
                    tpose(PK[bkx], PS[bkx][:, 0:128], x1buf[:, c8 * 128:(c8 + 1) * 128],
                          ["h2t"])
                    vx(lambda e, c8=c8, bkx=bkx: e.tensor_copy(
                        scr[:, c8 * 128:(c8 + 1) * 128], PS[bkx][:, 0:128]),
                       [PK[bkx]], ["scr"])
                for dch in range(8):
                    mm(5, PS[5][0:8, 0:128], wS[1][:, dch * 8:(dch + 1) * 8],
                       scr[:, dch * 128:(dch + 1) * 128],
                       ["wS1", "scr"], start=(dch == 0), stop=(dch == 7))
                vx(lambda e: e.tensor_copy(oev[0:8, 0:128], PS[5][0:8, 0:128]),
                   [PK[5]], ["oev"])
                dma("gpsimd", LGIN[:, i * 128:(i + 1) * 128], oev[0:8, 0:128],
                    ["oev"], ["lgin"])
        collective("AllGather", ALU.bypass, ALL_GROUP, C2IN[:, :], HTAB[0:B * S, :],
                   ["c2in"], ["HTAB"])
        collective("AllGather", ALU.bypass, ALL_GROUP, LGIN[:, :], LGOUT[:, :],
                   ["lgin"], ["LGOUT"])
        dma("gpsimd", HTAB[B * S:B * S + 128, :], C["zeros"][:], [], ["HTABpad"])

    def phase2():
        T.fence()
        dma("sync", tokc[:], TOKC[:], [], ["tokc"])
        dma("sync", tokcf[:], TOKCF[:], [], ["tokcf"])
        dma("sync", iotat[:], C["iota8"][:], [], ["iotat"])
        vx(lambda e: e.memset(scr[:, 0:64], float(B * S)), [], ["scr"])
        vx(lambda e: e.memset(scr[:, 1:64:2], 0.0), ["scr"], ["scr"])
        for rep_i in range(9):
            r0 = rep_i * 4096
            n_ = min(4096, NE * CAP + 256 - r0)
            dma("gpsimd",
                SLOT[r0:r0 + n_, :].rearrange("(a b) c -> a (b c)", b=n_ // 128),
                scr[:, 0:2 * (n_ // 128)], ["scr"], ["SLOT"])
        for zi in range(B * S // 128 + 1):
            dma("gpsimd", ZTAB[zi * 128:(zi + 1) * 128, :], C["zeros"][:],
                [], ["ZTAB"])

        T.fence()
        dma("sync", xt[3][:, 0:128], C["tril"][:], [], ["trilS"])
        dma("sync", xt[3][:, 128:256], C["triu1"][:], [], ["trilS"])
        for tl in range(128):
            sh_ = tl // 16
            if tl % 16 == 0:
                dma("sync", pbuf[0:8, 0:2048], LGOUT[sh_ * 8:(sh_ + 1) * 8, :],
                    ["LGOUT"], ["pbufL"])
            col = (tl % 16) * 128
            tpose(PK[6], PS[6][:, 0:8], pbuf[0:8, col:col + 128], ["pbufL"], np_=8)
            vx(lambda e: e.tensor_copy(scr[:, 0:8], PS[6][:, 0:8]),
               [PK[6]], ["ltT"])
            vx(lambda e: e.max(stat[:, 0:8], scr[:, 0:8]), ["ltT"], ["mx8"])
            vx(lambda e: e.max_index(ramu[:, 0:8], stat[:, 0:8], scr[:, 0:8]),
               ["mx8", "ltT"], ["ami"])
            vx(lambda e: e.tensor_copy(stat[:, 8:10], ramu[:, 0:2]),
               ["ami"], ["idxf"])
            vx(lambda e: e.tensor_scalar_mul(stat[:, 10:11], stat[:, 0:1], -1.0),
               ["mx8"], ["negm"])
            vx(lambda e: e.activation(scr[:, 8:16], scr[:, 0:8], AF.Exp,
                                      bias=stat[:, 10:11],
                                      accum_out=stat[:, 11:12]),
               ["ltT", "negm"], ["scrE", "sume"], eng="scalar")
            vx(lambda e: e.reciprocal(stat[:, 12:13], stat[:, 11:12]),
               ["sume"], ["p1c"])
            vx(lambda e: e.tensor_sub(stat[:, 13:14], stat[:, 1:2], stat[:, 0:1]),
               ["mx8"], ["d10"])
            vx(lambda e: e.activation(stat[:, 14:15], stat[:, 13:14], AF.Exp),
               ["d10"], ["e10"], eng="scalar")
            vx(lambda e: e.tensor_mul(stat[:, 15:16], stat[:, 14:15], stat[:, 12:13]),
               ["e10", "p1c"], ["p2c"])
            vx(lambda e: e.tensor_scalar(scr[:, 16:24], iotat[:], stat[:, 8:9],
                                         None, ALU.is_equal, ALU.bypass),
               ["iotat", "idxf"], ["m1t"])
            vx(lambda e: e.tensor_scalar(scr[:, 24:32], iotat[:], stat[:, 9:10],
                                         None, ALU.is_equal, ALU.bypass),
               ["iotat", "idxf"], ["m2t"])
            first = (tl == 0)
            mm(4, PS[4][:, 0:8], xt[3][:, 0:128], scr[:, 16:24],
               ["trilS", "m1t"], start=first, stop=False)
            vx(lambda e: e.tensor_copy(scr[:, 32:40], PS[4][:, 0:8]),
               [PK[4]], ["cum1t"])
            mm(4, PS[4][:, 0:8], xt[3][:, 128:256], scr[:, 16:24],
               ["trilS", "m1t", "cum1t"], start=False, stop=(tl == 127))
            mm(5, PS[5][:, 0:8], xt[3][:, 0:128], scr[:, 24:32],
               ["trilS", "m2t"], start=first, stop=False)
            vx(lambda e: e.tensor_copy(scr[:, 40:48], PS[5][:, 0:8]),
               [PK[5]], ["cum2t"])
            mm(5, PS[5][:, 0:8], xt[3][:, 128:256], scr[:, 24:32],
               ["trilS", "m2t", "cum2t"], start=False, stop=(tl == 127))
            vx(lambda e: e.tensor_mul(scr[:, 48:56], scr[:, 16:24], scr[:, 32:40]),
               ["m1t", "cum1t"], ["pr1"])
            vx(lambda e, tl=tl: e.reduce_sum(otm[:, tl * 6:tl * 6 + 1],
                                             scr[:, 48:56], axis=AX.X),
               ["pr1"], ["colA"])
            vx(lambda e: e.tensor_mul(scr[:, 48:56], scr[:, 24:32], scr[:, 40:48]),
               ["m2t", "cum2t"], ["pr2"])
            vx(lambda e, tl=tl: e.reduce_sum(otm[:, tl * 6 + 1:tl * 6 + 2],
                                             scr[:, 48:56], axis=AX.X),
               ["pr2"], ["colA"])
            vx(lambda e, tl=tl: e.tensor_copy(otm[:, tl * 6 + 2:tl * 6 + 4],
                                              stat[:, 8:10]),
               ["idxf"], ["colA"])
            vx(lambda e, tl=tl: e.tensor_copy(otm[:, tl * 6 + 4:tl * 6 + 5],
                                              stat[:, 12:13]),
               ["p1c"], ["colA"])
            vx(lambda e, tl=tl: e.tensor_copy(otm[:, tl * 6 + 5:tl * 6 + 6],
                                              stat[:, 15:16]),
               ["p2c"], ["colA"])
        vx(lambda e: e.tensor_scalar_add(x1buf[:, 0:8], PS[4][:, 0:8], -1.0),
           [PK[4]], ["tot1m"])
        T.fence()
        for tl in range(128):
            cb = tl * 6
            vx(lambda e, cb=cb: e.tensor_scalar(scr[:, 0:8], iotat[:],
                                                otm[:, cb + 3:cb + 4], None,
                                                ALU.is_equal, ALU.bypass),
               ["iotat", "colA"], ["m2b"])
            vx(lambda e, cb=cb: e.tensor_scalar_add(stat[:, 0:1],
                                                    otm[:, cb:cb + 1], -1.0),
               ["colA"], ["s1c"])
            vx(lambda e: e.tensor_mul(scr[:, 8:16], scr[:, 0:8], x1buf[:, 0:8]),
               ["m2b", "tot1m"], ["selt"])
            vx(lambda e: e.reduce_sum(stat[:, 1:2], scr[:, 8:16], axis=AX.X),
               ["selt"], ["selt2"])
            vx(lambda e, cb=cb: e.tensor_add(stat[:, 2:3], otm[:, cb + 1:cb + 2],
                                             stat[:, 1:2]),
               ["colA", "selt2"], ["s2c"])
            vx(lambda e: e.tensor_single_scalar(stat[:, 3:4], stat[:, 0:1],
                                                float(CAP), ALU.is_lt),
               ["s1c"], ["v1c"])
            vx(lambda e: e.tensor_single_scalar(stat[:, 4:5], stat[:, 2:3],
                                                float(CAP), ALU.is_lt),
               ["s2c"], ["v2c"])
            vx(lambda e, cb=cb: e.tensor_mul(stat[:, 5:6], otm[:, cb + 4:cb + 5],
                                             stat[:, 3:4]),
               ["colA", "v1c"], ["t1c"])
            vx(lambda e, cb=cb: e.tensor_mul(stat[:, 6:7], otm[:, cb + 5:cb + 6],
                                             stat[:, 4:5]),
               ["colA", "v2c"], ["t2c"])
            vx(lambda e: e.tensor_add(stat[:, 7:8], stat[:, 5:6], stat[:, 6:7]),
               ["t1c", "t2c"], ["denc"])
            vx(lambda e: e.tensor_single_scalar(stat[:, 7:8], stat[:, 7:8],
                                                EPS32, ALU.max),
               ["denc"], ["denc"])
            vx(lambda e: e.reciprocal(stat[:, 8:9], stat[:, 7:8]),
               ["denc"], ["rdenc"])
            vx(lambda e: e.tensor_mul(sgg[:, 2:3], stat[:, 5:6], stat[:, 8:9]),
               ["t1c", "rdenc"], ["sggT"])
            vx(lambda e: e.tensor_mul(sgg[:, 3:4], stat[:, 6:7], stat[:, 8:9]),
               ["t2c", "rdenc"], ["sggT"])
            for kk in range(2):
                vx(lambda e, cb=cb, kk=kk: e.tensor_scalar(
                    stat[:, 9 + kk:10 + kk], otm[:, cb + 2 + kk:cb + 3 + kk],
                    float(CAP), None, ALU.mult, ALU.bypass),
                   ["colA"], [f"scA{kk}"])
                vx(lambda e, kk=kk: e.tensor_add(
                    stat[:, 9 + kk:10 + kk], stat[:, 9 + kk:10 + kk],
                    stat[:, 0 + 2 * kk:1 + 2 * kk]),
                   [f"scA{kk}", "s1c" if kk == 0 else "s2c"], [f"scA{kk}"])
                vx(lambda e, kk=kk: e.tensor_scalar(
                    stat[:, 11 + kk:12 + kk], stat[:, 3 + kk:4 + kk],
                    -1.0e6, 1.0e6, ALU.mult, ALU.add),
                   ["v1c" if kk == 0 else "v2c"], [f"nvB{kk}"])
                vx(lambda e, kk=kk: e.tensor_add(
                    stat[:, 9 + kk:10 + kk], stat[:, 9 + kk:10 + kk],
                    stat[:, 11 + kk:12 + kk]),
                   [f"scA{kk}", f"nvB{kk}"], [f"scA{kk}"])
                vx(lambda e, kk=kk: e.tensor_single_scalar(
                    stat[:, 9 + kk:10 + kk], stat[:, 9 + kk:10 + kk],
                    float(NE * CAP), ALU.min),
                   [f"scA{kk}"], [f"scA{kk}"])
            vx(lambda e: e.tensor_copy(sidx[:, 0:2], stat[:, 9:11]),
               ["scA0", "scA1"], ["sidxT"])
            vx(lambda e, tl=tl: e.tensor_copy(sgg[:, 0:1], tokcf[:, tl:tl + 1]),
               ["tokcf"], ["sggT"])
            vx(lambda e: e.tensor_copy(sgg[:, 1:2], sgg[:, 2:3]),
               ["sggT"], ["sggT"])
            for kk in range(2):
                if kk == 1:
                    vx(lambda e: e.tensor_copy(sgg[:, 1:2], sgg[:, 3:4]),
                       ["sggT"], ["sggT"])
                isc(SLOT, sidx[:, kk:kk + 1], sgg[:, 0:2],
                    ["sidxT", "sggT"], ["SLOT"])

        if "sdbg" in DBG:
            dma("gpsimd", DBG["sdbg"][0:64, 0:2], SLOT[16384:16448, :], ["SLOT"], ["dbgs"])
            dma("gpsimd", DBG["sdbg"][64:128, 0:2], SLOT[0:64, :], ["SLOT"], ["dbgs"])
            dma("gpsimd", DBG["sdbg"][128:256, 0:768], otm[:, 0:768], ["colA"], ["dbgs"])
            dma("gpsimd", DBG["sdbg"][256:384, 0:8], x1buf[:, 0:8], ["tot1m"], ["dbgs"])
        T.fence()
        dma("sync", brep2[:], W["moe_b2"][:], [], ["brep2"])
        for g in range(16):
            for k in range(2):
                ixc = IX[f"slot{g}_{k}"]
                ig(carry[:, 2 * k:2 * k + 2], SLOT, idxt[:, ixc:ixc + 1],
                   ["SLOT"], ["slotp"])
            vx(lambda e: e.tensor_copy(sidx[:, 0:1], carry[:, 0:1]),
               ["slotp"], ["sidxE"])
            vx(lambda e: e.tensor_copy(sidx[:, 1:2], carry[:, 2:3]),
               ["slotp"], ["sidxE"])
            vx(lambda e: e.tensor_copy(sgg[:, 0:1], carry[:, 1:2]),
               ["slotp"], ["sggE"])
            vx(lambda e: e.tensor_copy(sgg[:, 1:2], carry[:, 3:4]),
               ["slotp"], ["sggE"])
            for k in range(2):
                ig(xt[k][:], HTAB, sidx[:, k:k + 1], ["HTAB", "sidxE"], [f"xt{k}"])
            for c8 in range(8):
                bk = 6 + (c8 % 2)
                for t in range(2):
                    tpose(PK[bk], PS[bk][:, t * 128:(t + 1) * 128],
                          xt[t][:, c8 * 128:(c8 + 1) * 128], [f"xt{t}"])
                vx(lambda e, c8=c8, bk=bk: e.tensor_copy(
                    xf[:, c8 * 512:c8 * 512 + 256], PS[bk][:, 0:256]),
                   [PK[bk]], [f"xf{c8}"])
            vx(lambda e: e.tensor_scalar_mul(sgg[:, 2:4], sgg[:, 0:2], 0.8),
               ["sggE"], ["ggE"])
            for fgrp in range(8):
                slot_w = fgrp % 2
                dma("sync", wS[slot_w][:, 0:4096].rearrange("p (dc n) -> p dc n", dc=8),
                    W["moe_w1"][:, fgrp * 512:(fgrp + 1) * 512].rearrange(
                        "(dc p) n -> p dc n", p=128),
                    ["moe_w1"], [f"wS{slot_w}"])
                dma("sync", ropes[:], W["moe_b1"][:, fgrp * 512:(fgrp + 1) * 512],
                    ["moe_b1"], ["b1s"])
                for sub in range(2):
                    bkh = 6 + sub
                    for dch in range(8):
                        mm(bkh, PS[bkh][:],
                           xf[:, dch * 512 + sub * 128: dch * 512 + (sub + 1) * 128],
                           wS[slot_w][:, dch * 512:(dch + 1) * 512],
                           [f"wS{slot_w}", f"xf{dch}"], start=(dch == 0), stop=(dch == 7))
                    vx(lambda e, sub=sub, bkh=bkh: e.tensor_add(
                        evac[sub][:], PS[bkh][:], ropes[:]),
                       [PK[bkh], "b1s"], [f"evac{sub}"])
                    vx(lambda e, sub=sub: e.tensor_relu(evac[sub][:], evac[sub][:]),
                       [f"evac{sub}"], [f"evac{sub}"])
                for fsc in range(4):
                    for sub in range(2):
                        tpose(PK[5], PS[5][:, sub * 128:(sub + 1) * 128],
                              evac[sub][:, fsc * 128:(fsc + 1) * 128],
                              [f"evac{sub}"])
                    vx(lambda e, fsc=fsc: e.tensor_copy(attn[:, fsc * 256:(fsc + 1) * 256],
                                                        PS[5][:, 0:256]),
                       [PK[5]], ["hfm"])
                dma("sync", pbuf[:].rearrange("p (dc n) -> p dc n", dc=4),
                    W["moe_w2"][fgrp * 512:(fgrp + 1) * 512, :].rearrange(
                        "(dc p) n -> p dc n", p=128),
                    ["moe_w2"], ["wbufy"])
                for fsc in range(4):
                    for sub in range(2):
                        for dh in range(2):
                            mm(sub * 2 + dh, PS[sub * 2 + dh][:],
                               attn[:, fsc * 256 + sub * 128: fsc * 256 + (sub + 1) * 128],
                               pbuf[:, fsc * 1024 + dh * 512: fsc * 1024 + (dh + 1) * 512],
                               ["hfm", "wbufy"],
                               start=(fgrp == 0 and fsc == 0),
                               stop=(fgrp == 7 and fsc == 3))
            for sub in range(2):
                for dh in range(2):
                    bky = sub * 2 + dh
                    vx(lambda e, bky=bky, dh=dh: e.tensor_add(
                        scr[:, dh * 512:(dh + 1) * 512], PS[bky][:],
                        brep2[:, dh * 512:(dh + 1) * 512]),
                       [PK[bky], "brep2"], ["scr"])
                vx(lambda e, sub=sub: e.tensor_scalar(
                    x1buf[:], scr[:, 0:1024], sgg[:, 2 + sub:3 + sub], None,
                    ALU.mult, ALU.bypass), ["scr", "ggE"], ["yout"])
                isc(ZTAB, sidx[:, sub:sub + 1], x1buf[:], ["yout", "sidxE"], ["ZTAB"])
        if "zdbg" in DBG:
            dma("gpsimd", DBG["zdbg"][0:8, :], ZTAB[0:8, :], ["ZTAB"], ["dbgz"])
        collective("ReduceScatter", ALU.add, ALL_GROUP, ZTAB[0:B * S, :], MOE[:, :],
                   ["ZTAB", "dbgz"], ["MOE"])

        T.fence()
        for i in range(16):
            dma("sync", xt[0][:], XMID[i * 128:(i + 1) * 128, :], ["XMID"], ["xt0"])
            dma("sync", xt[1][:], MOE[i * 128:(i + 1) * 128, :], ["MOE"], ["xt1"])
            if "moedbg" in DBG and i < 8:
                dma("gpsimd", DBG["moedbg"][i * 128:(i + 1) * 128, :], xt[1][:],
                    ["xt1"], ["dbgmoe"])
            ig(xt[2][:], X1F, idxt[:, IX[f"x1o{i}"]:IX[f"x1o{i}"] + 1], ["X1F"], ["xt2"])
            vx(lambda e: e.tensor_add(attn[:], xt[0][:], xt[1][:]),
               ["xt0", "xt1"], ["attn"])
            vx(lambda e: e.tensor_add(attn[:], attn[:], xt[2][:]),
               ["attn", "xt2"], ["attn"])
            layernorm_tile(attn[:], x1buf[:], ["attn"], "x2t")
            dma("gpsimd", C4IN[i * 128:(i + 1) * 128, :], x1buf[:], ["x2t"], ["c4in"])
            if "x2own" in DBG:
                dma("gpsimd", DBG["x2own"][i * 128:(i + 1) * 128, :], x1buf[:],
                    ["x2t"], ["dbgx2"])
        if "vdbg" in DBG:
            dma("gpsimd", DBG["vdbg"][896:1024, :], C4IN[0:128, :], ["c4in"], ["dbgv4"])
        collective("AllGather", ALU.bypass, PAIR_GROUPS, C4IN[:, :], X2F[:, :],
                   ["c4in", "dbgv4"], ["X2F"])


    def bcast01(dst, src32, rk, wk):
        zm = [0] * 32
        vx(lambda e: e.stream_shuffle(dst[0:32, :], src32, zm), rk, [wk])
        for qd in range(1, 4):
            vx(lambda e, qd=qd: e.stream_shuffle(
                dst[qd * 32:(qd + 1) * 32, :], dst[0:32, :], zm), [wk], [wk])

    def phase3():
        T.fence()
        if "vdbg" in DBG:
            dma("gpsimd", DBG["vdbg"][512:640, :], X2F[0:128, :], ["X2F"], ["dbgv2"])
            dma("gpsimd", DBG["vdbg"][640:768, :], C4IN[0:128, :], ["c4in"], ["dbgv2"])
        dma("sync", onescol[:], C["onesc"][:], [], ["ones"])
        dma("sync", eps10[:], C["eps10"][:], [], ["eps10"])
        dma("sync", b2s[:], W["l1a_b2"][:, 0:1], [], ["b2s"])
        for t in range(3):
            dma("sync", bandt[:, t * 128:(t + 1) * 128], C[f"band{t}"][:], [], ["bandc"])
        dma("sync", b1cl[:], B1CL[:], [], ["b1cl"])
        dma("sync", w2tc[:], W["l1a_w2T"][:], ["l1a_w2T"], ["w2tc"])
        load_w_resident(0, W["l1a_wprojT"], "l1a_wprojT")
        dma("sync", brep[0][:], W["l1a_bproj"][:], [], ["brep0"])
        vx(lambda e: e.memset(stat[:, 0:1], 0.0), [], ["gacc"])
        for g in range(8):
            cb = g * 512
            fm_group(X2F, [cb + t * 128 for t in range(4)], "X2F")
            for tsub in range(4):
                j = g * 4 + tsub
                for oc in range(2):
                    bk = (2 * tsub + oc) % 4
                    for dch in range(8):
                        mm(bk, PS[bk][:],
                           xf[:, dch * 512 + tsub * 128: dch * 512 + (tsub + 1) * 128],
                           wchunk(0, dch, oc * 512, (oc + 1) * 512),
                           ["wS0", f"xf{dch}"], start=(dch == 0), stop=(dch == 7))
                    vx(lambda e, oc=oc, bk=bk: e.tensor_add(
                        attn[:, oc * 512:(oc + 1) * 512], PS[bk][:],
                        brep[0][:, oc * 512:(oc + 1) * 512]),
                       [PK[bk], "brep0"], ["attn"])
                dma("gpsimd", INV[j * 128:(j + 1) * 128, :], attn[:],
                    ["attn"], ["INV"])
                vx(lambda e, tsub=tsub: e.tensor_sub(scr[:, 0:1024], xt[tsub][:],
                                                     attn[:]),
                   [f"xt{tsub}", "attn"], ["scr"])
                vx(lambda e: e.activation(x1buf[:], scr[:, 0:1024], AF.Square,
                                          accum_out=lstat[:, 0:1]),
                   ["scr"], ["x1buf", "gq1"], eng="scalar")
                vx(lambda e, tsub=tsub: e.activation(x1buf[:], xt[tsub][:], AF.Square,
                                                     accum_out=lstat[:, 1:2]),
                   [f"xt{tsub}"], ["x1buf", "gq2"], eng="scalar")
                vx(lambda e: e.activation(lstat[:, 2:3], lstat[:, 0:1], AF.Sqrt),
                   ["gq1"], ["gs1"], eng="scalar")
                vx(lambda e: e.activation(lstat[:, 3:4], lstat[:, 1:2], AF.Sqrt),
                   ["gq2"], ["gs2"], eng="scalar")
                vx(lambda e: e.tensor_scalar_add(lstat[:, 3:4], lstat[:, 3:4], 1.0e-8),
                   ["gs2"], ["gs2"])
                vx(lambda e: e.reciprocal(lstat[:, 4:5], lstat[:, 3:4]),
                   ["gs2"], ["gr"])
                vx(lambda e: e.tensor_mul(lstat[:, 5:6], lstat[:, 2:3], lstat[:, 4:5]),
                   ["gs1", "gr"], ["grat"])
                vx(lambda e, j=j: e.scalar_tensor_tensor(
                    out=stat[:, 0:1], in0=lstat[:, 5:6],
                    scalar=offf[:, 34 + j:35 + j], in1=stat[:, 0:1],
                    op0=ALU.mult, op1=ALU.add),
                   ["grat", "offf", "gacc"], ["gacc"])
        mm(6, PS[6][0:1, 0:1], onescol[:], stat[:, 0:1], ["ones", "gacc"],
           start=True, stop=True)
        dma("gpsimd", AR1I[:], C["zeros"][0:16, 0:128], [], ["ar1i"])
        vx(lambda e: e.memset(scr[0:1, 0:128], 0.0), ["scr"], ["scr"])
        vx(lambda e: e.tensor_copy(scr[0:1, 0:1], PS[6][0:1, 0:1]),
           [PK[6], "scr"], ["gpart"])
        dma("gpsimd", AR1I[0:1, :], scr[0:1, 0:128], ["gpart"], ["ar1i"])
        collective("AllReduce", ALU.add, ALL_GROUP, AR1I[:, :], AR1O[:, :],
                   ["ar1i"], ["ar1o"])
        dma("sync", scr[0:1, 0:128], AR1O[0:1, :], ["ar1o"], ["scr"])
        vx(lambda e: e.tensor_scalar(lstat[0:1, 6:7], scr[0:1, 0:1],
                                     1.0 / (B * S), 0.3, ALU.mult, ALU.is_gt),
           ["scr"], ["gw"])
        bcast01(wrepb[:, 0:1], lstat[0:32, 6:7], ["gw"], "wrep")
        if "vdbg" in DBG:
            dma("gpsimd", DBG["vdbg"][768:896, :], X2F[0:128, :], ["X2F"], ["dbgv3"])
        T.fence()
        for j in range(NBLK):
            dma("sync", xt[0][:], X2F[j * 128:(j + 1) * 128, :], ["X2F"], ["xt0"])
            dma("sync", xt[1][:], INV[j * 128:(j + 1) * 128, :], ["INV"], ["xt1"])
            vx(lambda e: e.tensor_sub(scr[:, 0:1024], xt[1][:], xt[0][:]),
               ["xt0", "xt1"], ["scr"])
            vx(lambda e: e.scalar_tensor_tensor(
                out=scr[:, 0:1024], in0=scr[:, 0:1024], scalar=wrepb[:, 0:1],
                in1=xt[0][:], op0=ALU.mult, op1=ALU.add),
               ["scr", "wrep", "xt0"], ["scr"])
            dma("gpsimd", XQK[j * 128:(j + 1) * 128, :], scr[:, 0:1024],
                ["scr"], ["XQK"])
            if "xqkf" in DBG and j < 16:
                dma("gpsimd", DBG["xqkf"][j * 128:(j + 1) * 128, :], scr[:, 0:1024],
                    ["scr"], ["dbgxqk"])
        T.fence()
        load_w_resident(0, W["l1a_wkT"], "l1a_wkT")
        dma("sync", brep[0][:], W["l1a_bk"][:], [], ["brep0"])
        for g in range(8):
            cb = g * 512
            fm_group(XQK, [cb + t * 128 for t in range(4)], "XQK")
            for tsub in range(4):
                j = g * 4 + tsub
                for oc in range(2):
                    bk = (2 * tsub + oc) % 4
                    for dch in range(8):
                        mm(bk, PS[bk][:],
                           xf[:, dch * 512 + tsub * 128: dch * 512 + (tsub + 1) * 128],
                           wchunk(0, dch, oc * 512, (oc + 1) * 512),
                           ["wS0", f"xf{dch}"], start=(dch == 0), stop=(dch == 7))
                    vx(lambda e, oc=oc, bk=bk: e.tensor_add(
                        attn[:, oc * 512:(oc + 1) * 512], PS[bk][:],
                        brep[0][:, oc * 512:(oc + 1) * 512]),
                       [PK[bk], "brep0"], ["attn"])
                dma("gpsimd", KT1[j * 128:(j + 1) * 128, :], attn[:], ["attn"], ["KT1"])
                if "ktf" in DBG and j < 16:
                    dma("gpsimd", DBG["ktf"][j * 128:(j + 1) * 128, :], attn[:],
                        ["attn"], ["dbgkt"])
        load_w_resident(0, W["l1a_wqT"], "l1a_wqT")
        dma("sync", brep[0][:], W["l1a_bq"][:], [], ["brep0"])
        for qg in range(4):
            for t in range(4):
                i = qg * 4 + t
                ig(xt[t][:], XQK, idxt[:, IX[f"lnq{i}"]:IX[f"lnq{i}"] + 1],
                   ["XQK"], [f"xt{t}"])
            for c8 in range(8):
                bk = 6 + (c8 % 2)
                for t in range(4):
                    tpose(PK[bk], PS[bk][:, t * 128:(t + 1) * 128],
                          xt[t][:, c8 * 128:(c8 + 1) * 128], [f"xt{t}"])
                vx(lambda e, c8=c8, bk=bk: e.tensor_copy(
                    xf[:, c8 * 512:(c8 + 1) * 512], PS[bk][:]), [PK[bk]], [f"xf{c8}"])
            for tsub in range(4):
                i = qg * 4 + tsub
                for oc in range(2):
                    bk = (2 * tsub + oc) % 4
                    for dch in range(8):
                        mm(bk, PS[bk][:],
                           xf[:, dch * 512 + tsub * 128: dch * 512 + (tsub + 1) * 128],
                           wchunk(0, dch, oc * 512, (oc + 1) * 512),
                           ["wS0", f"xf{dch}"], start=(dch == 0), stop=(dch == 7))
                    vx(lambda e, oc=oc, bk=bk: e.tensor_add(
                        attn[:, oc * 512:(oc + 1) * 512], PS[bk][:],
                        brep[0][:, oc * 512:(oc + 1) * 512]),
                       [PK[bk], "brep0"], ["attn"])
                dma("gpsimd", QT1[i * 128:(i + 1) * 128, :], attn[:], ["attn"], ["QT1"])
        load_w_resident(0, W["l1a_wvT"], "l1a_wvT")
        dma("sync", brep[0][:], W["l1a_bv"][:], [], ["brep0"])
        for vg in range(5):
            nt_ = 4 if vg < 4 else 2
            for t in range(nt_):
                tt = vg * 4 + t
                ig(xt[t][:], X2F, idxt[:, IX[f"vh{tt}"]:IX[f"vh{tt}"] + 1],
                   ["X2F"], [f"xt{t}"])
            for c8 in range(8):
                bk = 6 + (c8 % 2)
                for t in range(nt_):
                    tpose(PK[bk], PS[bk][:, t * 128:(t + 1) * 128],
                          xt[t][:, c8 * 128:(c8 + 1) * 128], [f"xt{t}"])
                vx(lambda e, c8=c8, bk=bk, nt_=nt_: e.tensor_copy(
                    xf[:, c8 * 512:c8 * 512 + nt_ * 128], PS[bk][:, 0:nt_ * 128]),
                   [PK[bk]], [f"xf{c8}"])
            for tsub in range(nt_):
                tt = vg * 4 + tsub
                for oc in range(2):
                    bk = (2 * tsub + oc) % 4
                    for dch in range(8):
                        mm(bk, PS[bk][:],
                           xf[:, dch * 512 + tsub * 128: dch * 512 + (tsub + 1) * 128],
                           wchunk(0, dch, oc * 512, (oc + 1) * 512),
                           ["wS0", f"xf{dch}"], start=(dch == 0), stop=(dch == 7))
                    vx(lambda e, oc=oc, bk=bk: e.tensor_add(
                        attn[:, oc * 512:(oc + 1) * 512], PS[bk][:],
                        brep[0][:, oc * 512:(oc + 1) * 512]),
                       [PK[bk], "brep0"], ["attn"])
                dma("gpsimd", VT2[tt * 128:(tt + 1) * 128, :], attn[:],
                    ["attn"], ["VT2"])
        if "vdbg" in DBG:
            dma("gpsimd", DBG["vdbg"][0:128, :], X2F[0:128, :], ["X2F"], ["dbgv"])
            dma("gpsimd", DBG["vdbg"][128:512, :], VT2[0:384, :], ["VT2"], ["dbgv"])
        T.fence()
        dma("gpsimd", AR2I[:, :], C["zeros"][0:64, 0:128], [], ["ar2i"])

        def norms_softmax(src_key, n_tiles, TMsrc, KNdst, SFdst, sum_banks, QS=False):
            for jt in range(n_tiles):
                dma("sync", xt[0][:], TMsrc[jt * 128:(jt + 1) * 128, :],
                    [src_key], ["xt0"])
                for hf in range(2):
                    mm(sum_banks[hf], PS[sum_banks[hf]][0:1, 0:512], onescol[:],
                       xt[0][:, hf * 512:(hf + 1) * 512], ["ones", "xt0"],
                       start=(jt == 0), stop=(jt == n_tiles - 1))
                vx(lambda e: e.activation(kf[:, 0:1024], xt[0][:], AF.Square),
                   ["xt0"], ["kfsq"], eng="scalar")
                for h in range(8):
                    vx(lambda e, h=h: e.reduce_sum(
                        lstat[:, 0:1], kf[:, h * 128:(h + 1) * 128], axis=AX.X),
                       ["kfsq"], ["nsq"])
                    vx(lambda e: e.activation(lstat[:, 1:2], lstat[:, 0:1], AF.Sqrt),
                       ["nsq"], ["nsr"], eng="scalar")
                    vx(lambda e: e.tensor_single_scalar(lstat[:, 1:2], lstat[:, 1:2],
                                                        1.0e-12, ALU.max),
                       ["nsr"], ["nsr"])
                    vx(lambda e: e.reciprocal(lstat[:, 2:3], lstat[:, 1:2]),
                       ["nsr"], ["nrc"])
                    vx(lambda e, h=h: e.tensor_scalar(
                        x1buf[:, h * 128:(h + 1) * 128], xt[0][:, h * 128:(h + 1) * 128],
                        lstat[:, 2:3], None, ALU.mult, ALU.bypass),
                       ["xt0", "nrc"], ["knbuf"])
                    vx(lambda e, h=h: e.reduce_max(
                        lstat[:, 3:4], xt[0][:, h * 128:(h + 1) * 128], axis=AX.X),
                       ["xt0"], ["sm"])
                    vx(lambda e: e.tensor_scalar_mul(lstat[:, 4:5], lstat[:, 3:4], -1.0),
                       ["sm"], ["smn"])
                    vx(lambda e, h=h: e.activation(
                        attn[:, h * 128:(h + 1) * 128], xt[0][:, h * 128:(h + 1) * 128],
                        AF.Exp, bias=lstat[:, 4:5], accum_out=lstat[:, 5:6]),
                       ["xt0", "smn"], ["ksbuf", "sml"], eng="scalar")
                    vx(lambda e: e.reciprocal(lstat[:, 6:7], lstat[:, 5:6]),
                       ["sml"], ["smr"])
                    vx(lambda e, h=h: e.tensor_scalar(
                        attn[:, h * 128:(h + 1) * 128], attn[:, h * 128:(h + 1) * 128],
                        lstat[:, 6:7], None, ALU.mult, ALU.bypass),
                       ["ksbuf", "smr"], ["ksbuf"])
                dma("gpsimd", KNdst[jt * 128:(jt + 1) * 128, :], x1buf[:],
                    ["knbuf"], ["KNdst" + str(QS)])
                for h in range(8):
                    bk = 6 + (h % 2)
                    tpose(PK[bk], PS[bk][:, 0:128], attn[:, h * 128:(h + 1) * 128],
                          ["ksbuf"])
                    vx(lambda e, h=h, bk=bk: e.tensor_copy(
                        scr[:, h * 128:(h + 1) * 128], PS[bk][:, 0:128]),
                       [PK[bk]], ["sfT"])
                dma("gpsimd",
                    SFdst[:, jt * 128:(jt + 1) * 128].rearrange(
                        "(hh p) n -> p hh n", p=128),
                    scr[:, 0:1024].rearrange("p (hh n) -> p hh n", hh=8),
                    ["sfT"], ["SFdst" + str(QS)])
        norms_softmax("KT1", 32, KT1, KNT, KSF, (2, 3))
        norms_softmax("QT1", 16, QT1, QNT, QSF, (4, 5), QS=True)
        if "qsf" in DBG:
            dma("gpsimd", DBG["qsf"][0:1024, :], QSF[:, 0:1024], ["SFdstTrue"], ["dbgqsf"])
            dma("gpsimd", DBG["qsf"][1024:2048, :], QSF[:, 1024:2048], ["SFdstTrue"], ["dbgqsf"])
        T.fence()
        vx(lambda e: e.tensor_copy(scr[0:1, 0:512], PS[4][0:1, 0:512]),
           [PK[4]], ["qsrow"])
        vx(lambda e: e.tensor_copy(scr[0:1, 512:1024], PS[5][0:1, 0:512]),
           [PK[5]], ["qsrow"])
        dma("gpsimd", QSB[:], scr[0:1, 0:1024], ["qsrow"], ["qsb"])
        dma("gpsimd", x1buf[0:8, 0:128], QSB.rearrange("(h n) -> h n", h=8),
            ["qsb"], ["qs8"])
        isc(AR2I, idxt[0:8, IX["arb"]:IX["arb"] + 1], x1buf[0:8, 0:128],
            ["qs8", "ar2i"], ["ar2i"])
        vx(lambda e: e.tensor_copy(scr[0:1, 0:512], PS[2][0:1, 0:512]),
           [PK[2]], ["ksrow"])
        vx(lambda e: e.tensor_copy(scr[0:1, 512:1024], PS[3][0:1, 0:512]),
           [PK[3]], ["ksrow"])
        dma("gpsimd", KSB[:], scr[0:1, 0:1024], ["ksrow"], ["ksb"])
        dma("gpsimd", vh[0:8, 0:128], KSB.rearrange("(h n) -> h n", h=8),
            ["ksb"], ["ks8"])

        T.fence()
        vwt = [otm, attn, x1buf]
        for i in range(16):
            for t in range(3):
                tt = i + t
                ig(xt[t][:], KNT, idxt[:, IX[f"vh{tt}"]:IX[f"vh{tt}"] + 1],
                   ["KNdstFalse"], [f"xt{t}"])
                dma("sync", vwt[t][:], VT2[(i + t) * 128:(i + t + 1) * 128, :],
                    ["VT2"], [f"vw{t}"])
            dma("sync", xt[3][:], QNT[i * 128:(i + 1) * 128, :], ["KNdstTrue"], ["xt3"])
            for h in range(8):
                tpose(PK[6], PS[6][:, 0:128], xt[3][:, h * 128:(h + 1) * 128], ["xt3"])
                vx(lambda e: e.tensor_copy(qf[:], PS[6][:, 0:128]), [PK[6]], ["qfL"])
                for t in range(3):
                    tpose(PK[7], PS[7][:, 0:128], xt[t][:, h * 128:(h + 1) * 128],
                          [f"xt{t}"])
                    vx(lambda e, t=t: e.tensor_copy(
                        kf[:, t * 128:(t + 1) * 128], PS[7][:, 0:128]),
                       [PK[7]], ["kfW"])
                mm(0, PS[0][:, 0:384], qf[:], kf[:, 0:384], ["qfL", "kfW"],
                   start=True, stop=True)
                vx(lambda e: e.tensor_scalar(scr[:, 0:384], PS[0][:, 0:384],
                                             -2.0, 2.0, ALU.mult, ALU.add),
                   [PK[0]], ["simU"])
                vx(lambda e: e.tensor_scalar(scr[:, 0:384], scr[:, 0:384],
                                             0.0, 4.0, ALU.max, ALU.min),
                   ["simU"], ["simU"])
                vx(lambda e: e.activation(scr[:, 0:384], scr[:, 0:384], AF.Sqrt),
                   ["simU"], ["simU"], eng="scalar")
                for t in range(3):
                    vx(lambda e, t=t: e.tensor_sub(
                        pbuf[:, t * 128:(t + 1) * 128], bandt[:, t * 128:(t + 1) * 128],
                        scr[:, t * 128:(t + 1) * 128]),
                       ["simU", "bandc"], ["pbw"])
                if i == 0:
                    vx(lambda e: e.tensor_scalar(
                        pbuf[:, 0:128], pbuf[:, 0:128], offf[:, OF["bndl"]:OF["bndl"] + 1],
                        None, ALU.add, ALU.bypass), ["pbw", "offf"], ["pbw"])
                if i == 15:
                    vx(lambda e: e.tensor_scalar(
                        pbuf[:, 256:384], pbuf[:, 256:384], offf[:, OF["bndr"]:OF["bndr"] + 1],
                        None, ALU.add, ALU.bypass), ["pbw", "offf"], ["pbw"])
                vx(lambda e: e.reduce_max(stat[:, 0:1], pbuf[:, 0:384], axis=AX.X),
                   ["pbw"], ["bm0"])
                vx(lambda e: e.tensor_scalar_mul(stat[:, 1:2], stat[:, 0:1], -1.0),
                   ["bm0"], ["bm1"])
                vx(lambda e: e.activation(pbuf[:, 0:384], pbuf[:, 0:384], AF.Exp,
                                          bias=stat[:, 1:2], accum_out=stat[:, 2:3]),
                   ["pbw", "bm1"], ["pbw", "bl"], eng="scalar")
                vx(lambda e: e.tensor_scalar_add(stat[:, 2:3], stat[:, 2:3], 1.0e-10),
                   ["bl"], ["bl"])
                vx(lambda e: e.reciprocal(stat[:, 3:4], stat[:, 2:3]),
                   ["bl"], ["br"])
                vx(lambda e: e.tensor_scalar(pbuf[:, 0:384], pbuf[:, 0:384],
                                             stat[:, 3:4], None, ALU.mult, ALU.bypass),
                   ["pbw", "br"], ["pbw"])
                for t in range(3):
                    bk = 1 + (t % 2)
                    tpose(PK[bk], PS[bk][:, 0:128], pbuf[:, t * 128:(t + 1) * 128],
                          ["pbw"])
                    vx(lambda e, t=t, bk=bk: e.tensor_copy(
                        ptr[:, (t % 2) * 128:((t % 2) + 1) * 128], PS[bk][:, 0:128]),
                       [PK[bk]], [f"ptr{t % 2}"])
                    mm(5, PS[5][:, 0:128],
                       vwt[t][:, h * 128:(h + 1) * 128],
                       ptr[:, (t % 2) * 128:((t % 2) + 1) * 128],
                       [f"vw{t}", f"ptr{t % 2}"],
                       start=(t == 0), stop=(t == 2))
                vx(lambda e: e.tensor_copy(scr[:, 512:640], PS[5][:, 0:128]),
                   [PK[5]], ["opre"])
                dma("gpsimd", OPRE[h * 128:(h + 1) * 128, i * 128:(i + 1) * 128],
                    scr[:, 512:640], ["opre"], ["OPRE"])
                if "bdbg" in DBG and i == 0:
                    dma("gpsimd", DBG["bdbg"][h * 128:(h + 1) * 128, 0:384],
                        pbuf[:, 0:384], ["pbw"], ["dbgb"])
                    dma("gpsimd", DBG["bdbg"][1024 + h * 128:1024 + (h + 1) * 128, 0:128],
                        scr[:, 512:640], ["opre"], ["dbgb"])

        T.fence()
        vx(lambda e: e.memset(hjrow[0:1, 0:8], 0.0), [], ["hjacc"])
        for h in range(8):
            dma("sync", kf[:], KSF[h * 128:(h + 1) * 128, :], ["SFdstFalse"], ["kf"])
            for i in range(16):
                dma("sync", qf[:], QSF[h * 128:(h + 1) * 128, i * 128:(i + 1) * 128],
                    ["SFdstTrue"], ["qf"])
                for cc in range(8):
                    mm(cc % 4, PS[cc % 4][:], qf[:], kf[:, cc * 512:(cc + 1) * 512],
                       ["qf", "kf"], start=True, stop=True)
                    vx(lambda e, cc=cc: e.activation(
                        scr[:, 0:512], PS[cc % 4][:], AF.Ln, bias=eps10[:]),
                       [PK[cc % 4], "eps10"], ["jln"], eng="scalar")
                    vx(lambda e, cc=cc: e.tensor_mul(
                        x1buf[:, 0:512], PS[cc % 4][:], scr[:, 0:512]),
                       [PK[cc % 4], "jln"], ["jdump"])
                    vx(lambda e, cc=cc: e.reduce_sum(
                        stat[:, 16 + cc:17 + cc], x1buf[:, 0:512], axis=AX.X),
                       ["jdump"], [f"jac{cc}"])
                vx(lambda e: e.reduce_sum(stat[:, 24:25], stat[:, 16:24], axis=AX.X),
                   [f"jac{cc}" for cc in range(8)], ["jrow"])
                mm(6, PS[6][0:1, 0:1], onescol[:], stat[:, 24:25], ["ones", "jrow"],
                   start=(i == 0), stop=(i == 15))
            vx(lambda e, h=h: e.tensor_copy(hjrow[0:1, h:h + 1], PS[6][0:1, 0:1]),
               [PK[6]], ["hjacc"])
        vx(lambda e: e.memset(scr[0:2, 0:128], 0.0), ["scr", "jln", "jdump"], ["hjp"])
        vx(lambda e: e.tensor_copy(scr[0:1, 0:8], hjrow[0:1, 0:8]),
           ["hjacc", "hjp"], ["hjp"])
        isc(AR2I, idxt[0:2, IX["arb2"]:IX["arb2"] + 1], scr[0:2, 0:128],
            ["hjp", "ar2i"], ["ar2i"])
        collective("AllReduce", ALU.add, ALL_GROUP, AR2I[:, :], AR2O[:, :],
                   ["ar2i"], ["ar2o"])
        T.fence()
        ig(x1buf[0:8, 0:128], AR2O, idxt[0:8, IX["arb"]:IX["arb"] + 1],
           ["ar2o"], ["qs8f"])
        ig(scr[0:2, 0:128], AR2O, idxt[0:2, IX["arb2"]:IX["arb2"] + 1],
           ["ar2o", "hjp", "scr"], ["hjpre"])
        vx(lambda e: e.tensor_copy(hjrow[0:1, 8:16], scr[0:1, 0:8]),
           ["hjpre"], ["hj8"])

        def entropy8(src_ap, rk, outcol):
            vx(lambda e: e.reduce_max(lstat[0:8, 8:9], src_ap, axis=AX.X),
               rk, ["em"])
            vx(lambda e: e.tensor_scalar_mul(lstat[0:8, 9:10], lstat[0:8, 8:9], -1.0),
               ["em"], ["em2"])
            vx(lambda e: e.activation(scr[0:8, 0:128], src_ap, AF.Exp,
                                      bias=lstat[0:8, 9:10], accum_out=lstat[0:8, 10:11]),
               rk + ["em2", "hjpre", "hj8"], ["escr", "el"], eng="scalar")
            vx(lambda e: e.reciprocal(lstat[0:8, 11:12], lstat[0:8, 10:11]),
               ["el"], ["er"])
            vx(lambda e: e.tensor_scalar(scr[0:8, 0:128], scr[0:8, 0:128],
                                         lstat[0:8, 11:12], None, ALU.mult, ALU.bypass),
               ["escr", "er"], ["escr"])
            vx(lambda e: e.activation(scr[0:8, 128:256], scr[0:8, 0:128], AF.Ln,
                                      bias=eps10[0:8, :]), ["escr", "eps10"],
               ["eln"], eng="scalar")
            vx(lambda e: e.tensor_mul(scr[0:8, 256:384], scr[0:8, 0:128],
                                      scr[0:8, 128:256]),
               ["escr", "eln"], ["edump"])
            vx(lambda e, outcol=outcol: e.reduce_sum(
                lstat[0:8, outcol:outcol + 1], scr[0:8, 256:384], axis=AX.X),
               ["edump"], [f"ent{outcol}"])
        entropy8(x1buf[0:8, 0:128], ["qs8f"], 12)
        entropy8(vh[0:8, 0:128], ["ks8"], 13)
        dma("gpsimd", HJB[:], hjrow[0:1, 8:16], ["hj8"], ["hjb"])
        dma("sync", lstat[0:8, 14:15], HJB.rearrange("(h n) -> h n", h=8),
            ["hjb"], ["hjc"])
        vx(lambda e: e.tensor_add(lstat[0:8, 15:16], lstat[0:8, 12:13], lstat[0:8, 13:14]),
           ["ent12", "ent13"], ["hsum"])
        vx(lambda e: e.tensor_sub(lstat[0:8, 15:16], lstat[0:8, 15:16], lstat[0:8, 14:15]),
           ["hsum", "hjc"], ["hsum"])
        mm(6, PS[6][0:1, 0:1], onescol[0:8, :], lstat[0:8, 15:16], ["ones", "hsum"],
           start=True, stop=True)
        vx(lambda e: e.tensor_scalar_mul(lstat[0:1, 7:8], PS[6][0:1, 0:1], 0.125),
           [PK[6]], ["seval"])
        vx(lambda e: e.tensor_single_scalar(lstat[0:1, 0:1], lstat[0:1, 7:8],
                                            0.15, ALU.is_gt),
           ["seval"], ["gate1"])
        bcast01(wrepb[:, 1:2], lstat[0:32, 0:1], ["gate1"], "gaterep")

        load_w_resident(0, W["l1a_w1T"], "l1a_w1T", width=512)
        vx(lambda e: e.memset(stat[0:1, 4:5], 0.0), [], ["socacc"])
        for g in range(4):
            for dch in range(8):
                dma("sync", xf[:, dch * 512:(dch + 1) * 512],
                    OPRE[dch * 128:(dch + 1) * 128, g * 512:(g + 1) * 512],
                    ["OPRE"], [f"xf{dch}"])
            for oc in range(4):
                bk = oc % 2
                for dch in range(8):
                    mm(bk, PS[bk][:],
                       wS[0][:, dch * 512 + oc * 128: dch * 512 + (oc + 1) * 128],
                       xf[:, dch * 512:(dch + 1) * 512],
                       ["wS0", f"xf{dch}"], start=(dch == 0), stop=(dch == 7))
                vx(lambda e, oc=oc, bk=bk: e.tensor_scalar(
                    pbuf[:, oc * 512:(oc + 1) * 512], PS[bk][:],
                    wrepb[:, 1:2], b1cl[:, oc:oc + 1], ALU.mult, ALU.add),
                   [PK[bk], "gaterep", "b1cl"], ["h1g"])
                vx(lambda e, oc=oc: e.tensor_relu(
                    pbuf[:, oc * 512:(oc + 1) * 512], pbuf[:, oc * 512:(oc + 1) * 512]),
                   ["h1g"], ["h1g"])
            for oc in range(4):
                mm(4, PS[4][0:1, 0:512], w2tc[:, oc:oc + 1],
                   pbuf[:, oc * 512:(oc + 1) * 512], ["w2tc", "h1g"],
                   start=(oc == 0), stop=(oc == 3))
            vx(lambda e: e.activation(scr[0:1, 0:512], PS[4][0:1, 0:512], AF.Sigmoid,
                                      bias=b2s[0:1, 0:1], accum_out=lstat[0:1, 1:2]),
               [PK[4], "b2s"], ["curv", "cva"], eng="scalar")
            vx(lambda e: e.tensor_add(stat[0:1, 4:5], stat[0:1, 4:5], lstat[0:1, 1:2]),
               ["socacc", "cva"], ["socacc"])
        dma("gpsimd", AR3I[:], C["zeros"][0:16, 0:128], [], ["ar3i"])
        vx(lambda e: e.memset(scr[0:1, 0:128], 0.0), ["scr", "curv"], ["scr"])
        vx(lambda e: e.tensor_copy(scr[0:1, 0:1], stat[0:1, 4:5]),
           ["socacc", "scr"], ["socrow"])
        dma("gpsimd", AR3I[0:1, :], scr[0:1, 0:128], ["socrow"], ["ar3i"])
        collective("AllReduce", ALU.add, ALL_GROUP, AR3I[:, :], AR3O[:, :],
                   ["ar3i"], ["ar3o"])
        dma("sync", scr[0:1, 0:128], AR3O[0:1, :], ["ar3o"], ["scr"])
        vx(lambda e: e.tensor_scalar_mul(lstat[0:1, 2:3], scr[0:1, 0:1],
                                         -2.0 / (B * S)), ["scr"], ["soc1"])
        vx(lambda e: e.activation(lstat[0:1, 2:3], lstat[0:1, 2:3], AF.Exp),
           ["soc1"], ["soc1"], eng="scalar")
        vx(lambda e: e.tensor_scalar(lstat[0:1, 2:3], lstat[0:1, 2:3],
                                     0.5, 1.5, ALU.max, ALU.min), ["soc1"], ["soc1"])
        vx(lambda e: e.tensor_scalar(lstat[0:1, 2:3], lstat[0:1, 2:3],
                                     0.3, 0.7, ALU.mult, ALU.add), ["soc1"], ["soc1"])
        vx(lambda e: e.tensor_mul(lstat[0:1, 2:3], lstat[0:1, 2:3], lstat[0:1, 0:1]),
           ["soc1", "gate1"], ["comb"])
        bcast01(wrepb[:, 2:3], lstat[0:32, 2:3], ["comb"], "combrep")
        if "scal" in DBG:
            vx(lambda e: e.tensor_copy(scr[0:8, 0:1], lstat[0:8, 12:13]),
               ["ent12"], ["dscal"])
            vx(lambda e: e.tensor_copy(scr[0:8, 1:2], lstat[0:8, 13:14]),
               ["ent13"], ["dscal"])
            vx(lambda e: e.tensor_copy(scr[0:8, 2:3], lstat[0:8, 14:15]),
               ["hjc"], ["dscal"])
            vx(lambda e: e.tensor_copy(scr[0:1, 3:4], lstat[0:1, 7:8]),
               ["seval"], ["dscal"])
            vx(lambda e: e.tensor_copy(scr[0:1, 4:5], lstat[0:1, 0:1]),
               ["gate1"], ["dscal"])
            vx(lambda e: e.tensor_copy(scr[0:1, 5:6], lstat[0:1, 2:3]),
               ["comb"], ["dscal"])
            vx(lambda e: e.tensor_copy(scr[0:1, 6:7], lstat[0:1, 6:7]),
               ["gw"], ["dscal"])
            dma("gpsimd", DBG["scal"][0:8, 0:16], scr[0:8, 0:16], ["dscal"], ["dbgscal"])

        T.fence()
        load_w_resident(1, W["l1a_woT"], "l1a_woT")
        dma("sync", brep[0][:], W["l1a_bo"][:], [], ["brep0"])
        for g in range(4):
            for dch in range(8):
                dma("sync", xf[:, dch * 512:(dch + 1) * 512],
                    OPRE[dch * 128:(dch + 1) * 128, g * 512:(g + 1) * 512],
                    ["OPRE"], [f"xf{dch}"])
            for tsub in range(4):
                i = g * 4 + tsub
                for oc in range(2):
                    bk = (2 * tsub + oc) % 4
                    for dch in range(8):
                        mm(bk, PS[bk][:],
                           xf[:, dch * 512 + tsub * 128: dch * 512 + (tsub + 1) * 128],
                           wS[1][:, dch * D + oc * 512: dch * D + (oc + 1) * 512],
                           ["wS1", f"xf{dch}"], start=(dch == 0), stop=(dch == 7))
                    vx(lambda e, oc=oc, bk=bk: e.tensor_scalar(
                        attn[:, oc * 512:(oc + 1) * 512], PS[bk][:],
                        wrepb[:, 2:3], None, ALU.mult, ALU.bypass),
                       [PK[bk], "combrep"], ["attn"])
                vx(lambda e: e.tensor_add(attn[:], attn[:], brep[0][:]),
                   ["attn", "brep0"], ["attn"])
                ig(xt[0][:], X2F, idxt[:, IX[f"lnq{i}"]:IX[f"lnq{i}"] + 1],
                   ["X2F"], ["xt0"])
                vx(lambda e: e.tensor_add(attn[:], attn[:], xt[0][:]),
                   ["attn", "xt0"], ["attn"])
                layernorm_tile(attn[:], x1buf[:], ["attn"], "x3t")
                dma("gpsimd", X3[i * 128:(i + 1) * 128, :], x1buf[:], ["x3t"], ["X3"])
                if "x3own" in DBG:
                    dma("gpsimd", DBG["x3own"][i * 128:(i + 1) * 128, :], x1buf[:],
                        ["x3t"], ["dbgx3"])

    def phase4():
        T.fence()
        load_w_resident(0, W["l1f_wg"], "l1f_wg")
        dma("sync", brep[0][:], W["l1f_bg"][:], [], ["brep0"])
        for g in range(4):
            fm_group(X3, [(g * 4 + t) * 128 for t in range(4)], "X3")
            for c8 in range(8):
                dma("gpsimd", X3F[c8 * 128:(c8 + 1) * 128, g * 512:(g + 1) * 512],
                    xf[:, c8 * 512:(c8 + 1) * 512], [f"xf{c8}"], ["X3F"])
            for tsub in range(4):
                i = g * 4 + tsub
                for oc in range(2):
                    bk = (2 * tsub + oc) % 4
                    for dch in range(8):
                        mm(bk, PS[bk][:],
                           xf[:, dch * 512 + tsub * 128: dch * 512 + (tsub + 1) * 128],
                           wchunk(0, dch, oc * 512, (oc + 1) * 512),
                           ["wS0", f"xf{dch}"], start=(dch == 0), stop=(dch == 7))
                    vx(lambda e, oc=oc, bk=bk: e.tensor_add(
                        attn[:, oc * 512:(oc + 1) * 512], PS[bk][:],
                        brep[0][:, oc * 512:(oc + 1) * 512]),
                       [PK[bk], "brep0"], ["attn"])
                vx(lambda e: e.reduce_max(stat[:, 0:1], attn[:], axis=AX.X),
                   ["attn"], ["g0"])
                vx(lambda e: e.tensor_scalar_mul(stat[:, 1:2], stat[:, 0:1], -1.0),
                   ["g0"], ["g1"])
                vx(lambda e: e.activation(scr[:, 0:1024], attn[:], AF.Exp,
                                          bias=stat[:, 1:2], accum_out=stat[:, 2:3]),
                   ["attn", "g1"], ["scr", "g2"], eng="scalar")
                vx(lambda e, i=i: e.reciprocal(epcol[:, 2 * i:2 * i + 1], stat[:, 2:3]),
                   ["g2"], ["epc"])
                vx(lambda e: e.max(stat[:, 8:16], attn[:]), ["attn"], ["g3"])
                vx(lambda e: e.max_index(ramu[:, 0:8], stat[:, 8:16], attn[:]),
                   ["g3", "attn"], ["g4"])
                vx(lambda e: e.tensor_single_scalar(ramu[:, 0:1], ramu[:, 0:1],
                                                    7, ALU.bitwise_and),
                   ["g4"], ["g4"])
                vx(lambda e, i=i: e.tensor_copy(epcol[:, 2 * i + 1:2 * i + 2],
                                                ramu[:, 0:1]),
                   ["g4"], ["epc"])
        dma("sync", brep[0][:, 0:1024], W["l1f_be"][:, 0:1024], [], ["berep"])
        for g in range(4):
            vx(lambda e: e.memset(pbuf[:], 0.0), [], ["oacc"])
            for ex in range(8):
                dma("sync", wS[1][:].rearrange("p (dc n) -> p dc n", dc=8),
                    W["l1f_we"][ex * D:(ex + 1) * D, :].rearrange(
                        "(dc p) n -> p dc n", p=128), ["l1f_we"], ["wS1"])
                if ex == 0:
                    pass
                for dch in range(8):
                    dma("sync", xf[:, dch * 512:(dch + 1) * 512],
                        X3F[dch * 128:(dch + 1) * 128, g * 512:(g + 1) * 512],
                        ["X3F"], [f"xf{dch}"])
                for tsub in range(4):
                    i = g * 4 + tsub
                    for oc in range(2):
                        bk = (2 * tsub + oc) % 4
                        for dch in range(8):
                            mm(bk, PS[bk][:],
                               xf[:, dch * 512 + tsub * 128: dch * 512 + (tsub + 1) * 128],
                               wS[1][:, dch * D + oc * 512: dch * D + (oc + 1) * 512],
                               ["wS1", f"xf{dch}"], start=(dch == 0), stop=(dch == 7))
                        vx(lambda e, oc=oc, bk=bk, ex=ex: e.tensor_add(
                            scr[:, oc * 512:(oc + 1) * 512], PS[bk][:],
                            brep[0][:, oc * 512:(oc + 1) * 512]),
                           [PK[bk], "berep"], ["scr"])
                    vx(lambda e: e.tensor_relu(scr[:, 0:1024], scr[:, 0:1024]),
                       ["scr"], ["scr"])
                    vx(lambda e, i=i, ex=ex: e.tensor_single_scalar(
                        stat[:, 4:5], epcol[:, 2 * i + 1:2 * i + 2], float(ex),
                        ALU.is_equal), ["epc"], ["cmp"])
                    vx(lambda e, i=i: e.tensor_mul(stat[:, 5:6], stat[:, 4:5],
                                                   epcol[:, 2 * i:2 * i + 1]),
                       ["cmp", "epc"], ["mp"])
                    vx(lambda e, tsub=tsub: e.scalar_tensor_tensor(
                        out=pbuf[:, tsub * 1024:(tsub + 1) * 1024],
                        in0=scr[:, 0:1024], scalar=stat[:, 5:6],
                        in1=pbuf[:, tsub * 1024:(tsub + 1) * 1024],
                        op0=ALU.mult, op1=ALU.add),
                       ["scr", "mp", "oacc"], ["oacc"])
                if ex < 7:
                    dma("sync", brep[0][:, 0:1024],
                        W["l1f_be"][:, (ex + 1) * 1024:(ex + 2) * 1024], [], ["berep"])
            for tsub in range(4):
                i = g * 4 + tsub
                dma("sync", xt[0][:], X3[i * 128:(i + 1) * 128, :], ["X3"], ["xt0"])
                vx(lambda e, tsub=tsub: e.tensor_add(
                    attn[:], xt[0][:], pbuf[:, tsub * 1024:(tsub + 1) * 1024]),
                   ["xt0", "oacc"], ["attn"])
                layernorm_tile(attn[:], x1buf[:], ["attn"], "x4t")
                dma("gpsimd", OUT[i * 128:(i + 1) * 128, :], x1buf[:], ["x4t"], ["outw"])
            if g < 3:
                dma("sync", brep[0][:, 0:1024], W["l1f_be"][:, 0:1024], [], ["berep"])

    if NPHASE >= 1:
        phase0()
    if NPHASE >= 2:
        phase1()
    if NPHASE >= 3:
        phase2()
    if NPHASE >= 4:
        phase3()
    if NPHASE >= 5:
        phase4()

    if NPHASE < 5:
        for i in range(16):
            ig(otm[:, 0:1024], X1F, idxt[:, IX[f"x1o{i}"]:IX[f"x1o{i}"] + 1],
               ["X1F"], ["otm"])
            dma("gpsimd", OUT[i * 128:(i + 1) * 128, :], otm[:, 0:1024],
                ["otm"], ["outw"])

    T.finalize()
    T.realize(nc)
    p.stack.close()
    return p


def make_in_maps(inputs):
    consts = host_constants()

    def repv(v):
        v = np.asarray(v, np.float32).reshape(-1)
        return np.ascontiguousarray(np.broadcast_to(v, (128, v.size)))

    in_maps = []
    for c in range(8):
        b = c // 2
        ix, of = percore_tables(c)
        half = c % 2
        x0b = np.ascontiguousarray(np.asarray(inputs["L0_attn_x"], np.float32)[b])
        x0q = np.concatenate([x0b[j * 128:(j + 1) * 128] for j in L0_QB[half]], axis=0)
        rcq = np.concatenate([consts["ropec"][:, j * 128:(j + 1) * 128] for j in L0_QB[half]], axis=1)
        rsq = np.concatenate([consts["ropes"][:, j * 128:(j + 1) * 128] for j in L0_QB[half]], axis=1)
        m = {
            "x0": x0b,
            "x0q": np.ascontiguousarray(x0q),
            "ropecq": np.ascontiguousarray(rcq),
            "ropesq": np.ascontiguousarray(rsq),
            "l0a_wq": np.ascontiguousarray(inputs["L0_attn_wq"], np.float32),
            "l0a_wk": np.ascontiguousarray(inputs["L0_attn_wk"], np.float32),
            "l0a_wv": np.ascontiguousarray(inputs["L0_attn_wv"], np.float32),
            "l0a_wo": np.ascontiguousarray(inputs["L0_attn_wo_w"], np.float32),
            "l0a_wob_rep": repv(inputs["L0_attn_wo_b"]),
            "l0f_wq": np.ascontiguousarray(inputs["L0_ffn_wq"], np.float32),
            "l0f_wk": np.ascontiguousarray(inputs["L0_ffn_wk"], np.float32),
            "l0f_wv": np.ascontiguousarray(inputs["L0_ffn_wv"], np.float32),
            "l0f_wo": np.ascontiguousarray(inputs["L0_ffn_wo"], np.float32),
            "l0f_wr": np.ascontiguousarray(inputs["L0_ffn_wr"], np.float32),
            "l0f_bq_rep": repv(inputs["L0_ffn_bq"]),
            "l0f_bk_rep": repv(inputs["L0_ffn_bk"]),
            "l0f_bv_rep": repv(inputs["L0_ffn_bv"]),
            "l0f_bo_rep": repv(inputs["L0_ffn_bo"]),
            "l0f_g1_rep": repv(inputs["L0_ffn_ln1_g"]),
            "l0f_b1_rep": repv(inputs["L0_ffn_ln1_b"]),
            "l0f_g2_rep": repv(inputs["L0_ffn_ln2_g"]),
            "l0f_b2_rep": repv(inputs["L0_ffn_ln2_b"]),
            "l1a_wqT": np.ascontiguousarray(np.asarray(inputs["L1_attn_Wq"], np.float32).T),
            "l1a_wkT": np.ascontiguousarray(np.asarray(inputs["L1_attn_Wk"], np.float32).T),
            "l1a_wvT": np.ascontiguousarray(np.asarray(inputs["L1_attn_Wv"], np.float32).T),
            "l1a_woT": np.ascontiguousarray(np.asarray(inputs["L1_attn_Wo"], np.float32).T),
            "l1a_wprojT": np.ascontiguousarray(np.asarray(inputs["L1_attn_Wproj"], np.float32).T),
            "l1a_w1T": np.ascontiguousarray(np.asarray(inputs["L1_attn_W1"], np.float32).T),
            "l1a_w2T": np.ascontiguousarray(
                np.asarray(inputs["L1_attn_W2"], np.float32).T.reshape(4, 128).T),
            "l1a_bq_rep": repv(inputs["L1_attn_bq"]),
            "l1a_bk_rep": repv(inputs["L1_attn_bk"]),
            "l1a_bv_rep": repv(inputs["L1_attn_bv"]),
            "l1a_bo_rep": repv(inputs["L1_attn_bo"]),
            "l1a_bproj_rep": repv(inputs["L1_attn_bproj"]),
            "l1a_b1_rep": repv(inputs["L1_attn_b1"]),
            "l1a_b1_col": np.ascontiguousarray(
                np.asarray(inputs["L1_attn_b1"], np.float32).reshape(4, 128).T),
            "l1a_b2_rep": repv(inputs["L1_attn_b2"]),
            "l1f_wg": np.ascontiguousarray(inputs["L1_ffn_Wg"], np.float32),
            "l1f_bg_rep": repv(inputs["L1_ffn_bg"]),
            "l1f_we": np.ascontiguousarray(
                np.asarray(inputs["L1_ffn_We"], np.float32).reshape(NE * D, D)),
            "l1f_be_rep": repv(np.asarray(inputs["L1_ffn_be"], np.float32).reshape(-1)),
            "moe_w1": np.ascontiguousarray(np.asarray(inputs["L0_ffn_w1"], np.float32)[c]),
            "moe_w2": np.ascontiguousarray(np.asarray(inputs["L0_ffn_w2"], np.float32)[c]),
            "moe_b1_rep": repv(np.asarray(inputs["L0_ffn_b1"], np.float32)[c]),
            "moe_b2_rep": repv(np.asarray(inputs["L0_ffn_b2"], np.float32)[c]),
            "idxt": ix,
            "offf": of,
            "l0f_bq_col": np.ascontiguousarray(
                np.asarray(inputs["L0_ffn_bq"], np.float32).reshape(8, 128).T),
            "l0f_bk_col": np.ascontiguousarray(
                np.asarray(inputs["L0_ffn_bk"], np.float32).reshape(8, 128).T),
            "tokcol": np.ascontiguousarray(
                (np.arange(128)[:, None] + 128 * np.arange(128)[None, :]).astype(np.int32)),
            "tokcolf": np.ascontiguousarray(
                (np.arange(128)[:, None] + 128 * np.arange(128)[None, :]).astype(np.float32)),
        }
        for nm in ["ident", "ropec", "ropes", "t0", "band0", "band1", "band2",
                   "iota8", "zeros", "tril", "triu1", "onesc", "eps10"]:
            m["c_" + nm] = consts[nm]
        in_maps.append(m)
    return in_maps


def kernel(**inputs):
    in_maps = make_in_maps(inputs)
    prog = build_program()
    res = run_bass_kernel_spmd(prog.nc, in_maps, core_ids=list(range(8)))
    out = np.zeros((B, S, D), np.float32)
    for c in range(8):
        b, half = c // 2, c % 2
        out[b, half * SH:(half + 1) * SH, :] = res.results[c]["out"]
    return out


KERNEL_RESULTS = {}


def kernel_dev(inputs):
    in_maps = make_in_maps(inputs)
    prog = build_program()
    res = run_bass_kernel_spmd(prog.nc, in_maps, core_ids=list(range(8)))
    return res
